# version 18
# baseline (speedup 1.0000x reference)
import numpy as np
import contextlib
import concourse.bass as bass
import concourse.mybir as mybir
from concourse.bass_utils import run_bass_kernel_spmd

F32 = mybir.dt.float32
BF16 = mybir.dt.bfloat16
AX = mybir.AxisListType.X
OP = mybir.AluOpType
ACT = mybir.ActivationFunctionType
MAGIC = float(1.5 * 2.0**23)
NEG = -1e30

NCORES = 8
B = 2
S = 2048
NT = 16
D = 1024
DC = 8
E = 256
EC = 2
LOOK = 64
II = 3072
ICN = 24
TPC = 4096
NT32 = 32

ENGS = ("tensor", "vector", "scalar", "gpsimd", "sync")
ESEM = {"tensor": "tS", "vector": "vS", "scalar": "sS", "gpsimd": "gS", "sync": "yS"}
ALLSEMS = ("ld", "sd", "gd", "so", "cc", "tS", "vS", "sS", "gS", "yS",
           "xA", "xB", "wA", "wB", "pA", "pB")


class Sched:
    def __init__(self):
        self.ops = {e: [] for e in ENGS}
        self.cnt = {s: 0 for s in ALLSEMS}
        self.pending = {e: [] for e in ENGS}
        self.seq = 0
        self.cutoff = None

    def val(self, sem):
        return self.cnt[sem]

    def emit(self, eng, fn, waits=(), inc=None, amt=1):
        w = list(self.pending[eng])
        self.pending[eng] = []
        for x in waits:
            if x is None:
                continue
            s, v = x
            if s == ESEM[eng] or v <= 0:
                continue
            w.append((s, v))
        dd = {}
        for s, v in w:
            dd[s] = max(dd.get(s, 0), v)
        w = sorted(dd.items())
        newv = None
        if inc is not None:
            self.cnt[inc] += amt
            newv = self.cnt[inc]
        self.ops[eng].append((self.seq, w, inc, amt, fn))
        self.seq += 1
        return newv

    def mark(self, eng):
        return (ESEM[eng], self.val(ESEM[eng]))

    def barrier(self):
        marks = [(s, self.cnt[s]) for s in ("ld", "sd", "gd", "so", "tS", "vS", "sS", "gS",
                                            "xA", "xB", "wA", "wB", "pA", "pB")]
        for e in ENGS:
            self.pending[e] = [(s, v) for (s, v) in marks if s != ESEM[e] and v > 0]


class _Cut(Exception):
    pass


def build_nc(dbg=False, phase=99):
    nc = bass.Bass("TRN2", target_bir_lowering=False, debug=False)
    sch = Sched()
    def cut(k):
        if phase < k and sch.cutoff is None:
            sch.cutoff = sch.seq
    _uc = [0]
    def un(base):
        _uc[0] += 1
        return f"{base}_{_uc[0]}"

    d_x = nc.dram_tensor("x", [B, S, D], F32, kind="ExternalInput").ap()
    d_out = nc.dram_tensor("out", [B, S, D], F32, kind="ExternalOutput").ap()
    dw = {}
    for nm, shp in [
        ("wi", [D, E]), ("wq", [E, E]), ("wk", [E, E]), ("wv", [E, E]),
        ("wp", [E, E]), ("w1", [D, E]), ("w2", [E, E]), ("wo", [E, D]),
        ("wh2", [128, EC]), ("cwT", [E, 5]),
        ("bi", [128, EC]), ("bq", [128, EC]), ("bk", [128, EC]), ("bv1", [1, E]),
        ("bp", [128, EC]), ("b1", [128, EC]), ("b2", [128, EC]),
        ("bo1", [1, D]), ("cb", [128, EC]), ("bh2", [1, 1]),
        ("wup", [ICN, 128, D]), ("wgate", [ICN, 128, D]), ("wdown", [DC, 128, II]),
        ("ident", [128, 128]), ("r16", [128, S]),
    ]:
        dw[nm] = nc.dram_tensor(nm, shp, F32, kind="ExternalInput").ap()

    d_ln1 = nc.dram_tensor("ln1sp", [NT32, 128, D], F32, kind="Internal").ap()
    d_gr = nc.dram_tensor("grsp", [ICN, 128, TPC], F32, kind="Internal").ap()
    d_gp = nc.dram_tensor("gpsp", [ICN, 128, TPC], F32, kind="Internal").ap()
    d_op = nc.dram_tensor("opsp", [16, 128, ICN, 256], F32, kind="Internal").ap()
    d_wub = nc.dram_tensor("wubsp", [ICN, 128, DC * 128], BF16, kind="Internal").ap()
    cc_in = [nc.dram_tensor(f"cci{k}", [1, 2], F32, kind="Internal") for k in range(4)]
    cc_out = [nc.dram_tensor(f"cco{k}", [1, 2], F32, kind="Internal") for k in range(4)]
    if dbg:
        d_dbg_gum = nc.dram_tensor("dbg_gum", [2, 128, 192], F32, kind="ExternalOutput").ap()
        d_dbg_xr = nc.dram_tensor("dbg_xr", [3, 128, DC, 512], BF16, kind="ExternalOutput").ap()
        d_dbg_s2 = nc.dram_tensor("dbg_s2", [1, 16], F32, kind="ExternalOutput").ap()
        d_dbg_ln1 = nc.dram_tensor("dbg_ln1", [NT32, 128, D], F32, kind="ExternalOutput").ap()
        d_dbg_h = nc.dram_tensor("dbg_h", [B, 128, EC, S], F32, kind="ExternalOutput").ap()
        d_dbg_scal = nc.dram_tensor("dbg_scal", [1, 16], F32, kind="ExternalOutput").ap()
        d_dbg_or = nc.dram_tensor("dbg_or", [128, ICN, 256], BF16, kind="ExternalOutput").ap()
        d_dbg_ffn = nc.dram_tensor("dbg_ffn", [2, 128, D], F32, kind="ExternalOutput").ap()

    with contextlib.ExitStack() as ES:
        EE = ES.enter_context

        wi_bf = EE(nc.sbuf_tensor("wi_bf", [128, DC, E], BF16))
        wq_bf = EE(nc.sbuf_tensor("wq_bf", [128, EC, E], BF16))
        wk_bf = EE(nc.sbuf_tensor("wk_bf", [128, EC, E], BF16))
        wv_bf = EE(nc.sbuf_tensor("wv_bf", [128, EC, E], BF16))
        wp_bf = EE(nc.sbuf_tensor("wp_bf", [128, EC, E], BF16))
        w1_bf = EE(nc.sbuf_tensor("w1_bf", [128, DC, E], BF16))
        w1k_bf = EE(nc.sbuf_tensor("w1k_bf", [128, 5, EC, E], BF16))
        w2_bf = EE(nc.sbuf_tensor("w2_bf", [128, EC, E], BF16))
        wo_bf = EE(nc.sbuf_tensor("wo_bf", [128, EC, D], BF16))
        wh2_bf = EE(nc.sbuf_tensor("wh2_bf", [128, EC, 1], BF16))
        cwT_s = EE(nc.sbuf_tensor("cwT_s", [128, EC, 5], F32))
        ident_bf = EE(nc.sbuf_tensor("ident_bf", [128, 128], BF16))
        ident32 = EE(nc.sbuf_tensor("ident32", [128, 128], F32))
        r16_bf = EE(nc.sbuf_tensor("r16_bf", [128, S], BF16))
        ones_f = EE(nc.sbuf_tensor("ones_f", [1, 128], F32))
        bias = {nm: EE(nc.sbuf_tensor("bs_" + nm, [128, EC, 1], F32))
                for nm in ("bi", "bq4", "bk4", "b1e", "b2")}
        bh2_s = EE(nc.sbuf_tensor("bh2_s", [1, 1], F32))
        BV = EE(nc.sbuf_tensor("BV", [128, E], F32))
        BO = EE(nc.sbuf_tensor("BO", [128, D], F32))
        srow = EE(nc.sbuf_tensor("srow", [1, 16], F32))
        SCAL = EE(nc.sbuf_tensor("SCAL", [128, 16], F32))
        epsT = EE(nc.sbuf_tensor("epsT", [128, 1], F32))
        magicT = EE(nc.sbuf_tensor("magicT", [128, 1], F32))
        nmagicT = EE(nc.sbuf_tensor("nmagicT", [128, 1], F32))
        MASK = EE(nc.sbuf_tensor("MASK", [128, 192], F32))
        MASK0 = EE(nc.sbuf_tensor("MASK0", [128, 192], F32))
        ccs = EE(nc.sbuf_tensor("ccs", [1, 2], F32))
        foldt = EE(nc.sbuf_tensor("foldt", [128, 1], F32))

        sems = {s: EE(nc.semaphore(s)) for s in ALLSEMS if s != "yS"}
        sems["yS"] = EE(nc.semaphore("yS"))

        psBC = EE(nc.psum_tensor(un("psBC"), [128, 16], F32))
        psXP = EE(nc.psum_tensor(un("psXP"), [1, 128], F32))

        try:
            with contextlib.ExitStack() as esI:
                EI = esI.enter_context
                wstg = EI(nc.sbuf_tensor("wstg", [128, DC, E], F32))
                braw = {nm: EI(nc.sbuf_tensor("braw_" + nm, [128, EC, 1], F32))
                        for nm in ("bi", "bq", "bk", "bp", "b1", "b2", "cb")}
                bv1 = EI(nc.sbuf_tensor("bv1_s", [1, E], F32))
                bo1 = EI(nc.sbuf_tensor("bo1_s", [1, D], F32))
                w1f = EI(nc.sbuf_tensor("w1f", [128, 4, E], F32))
                psI = EI(nc.psum_tensor(un("psI"), [128, D], F32))
                psBE = EI(nc.psum_tensor(un("psBE"), [128, EC, 1], F32))

                cast_free = [None]

                def ld_cast(dsrc, cview, eview, bf_dst):
                    stg = wstg[:].rearrange("p c e -> p (c e)")[:, 0:cview * eview]
                    stg = stg.rearrange("p (c e) -> p c e", c=cview)
                    lv = sch.emit("sync", lambda e, d=dsrc, t=stg: e.dma_start(out=t, in_=d),
                                  waits=[cast_free[0]], inc="ld", amt=16)
                    c = sch.emit("vector", lambda e, t=stg, o=bf_dst: e.tensor_copy(out=o, in_=t),
                                 waits=[("ld", lv)], inc="vS")
                    cast_free[0] = ("vS", c)

                ld_cast(dw["wi"].rearrange("(c p) e -> p c e", p=128), DC, E, wi_bf[:])
                for nm, dst in (("wq", wq_bf), ("wk", wk_bf), ("wv", wv_bf),
                                ("wp", wp_bf), ("w2", w2_bf)):
                    ld_cast(dw[nm].rearrange("(c p) e -> p c e", p=128), EC, E, dst[:])
                ld_cast(dw["w1"].rearrange("(c p) e -> p c e", p=128), DC, E, w1_bf[:])
                ld_cast(dw["wo"].rearrange("(c p) d -> p c d", p=128), EC, D, wo_bf[:])
                ld_cast(dw["wh2"].rearrange("p (c o) -> p c o", c=EC), EC, 1, wh2_bf[:])
                ld_cast(dw["ident"].rearrange("p (c m) -> p c m", c=1), 1, 128,
                        ident_bf[:].rearrange("p (c m) -> p c m", c=1))
                ld_cast(dw["r16"].rearrange("p (c m) -> p c m", c=1), 1, S,
                        r16_bf[:].rearrange("p (c m) -> p c m", c=1))
                for dsrc, dst in [(dw["cwT"].rearrange("(c p) k -> p c k", p=128), cwT_s[:]),
                                  (dw["ident"], ident32[:]), (dw["bh2"], bh2_s[:]),
                                  (dw["bv1"], bv1[:]), (dw["bo1"], bo1[:])]:
                    sch.emit("sync", lambda e, d=dsrc, t=dst: e.dma_start(out=t, in_=d),
                             inc="ld", amt=16)
                for nm in braw:
                    sch.emit("sync", lambda e, d=dw[nm], t=braw[nm]: e.dma_start(
                        out=t[:], in_=d.rearrange("p (c o) -> p c o", o=1)), inc="ld", amt=16)
                lw1f = sch.emit("sync", lambda e: e.dma_start(
                    out=w1f[:], in_=dw["w1"].rearrange("(c p) e -> p c e", p=128)[:, 2:6, :]),
                    inc="ld", amt=16)
                all_ld = ("ld", sch.val("ld"))

                def vconsts(e):
                    e.memset(ones_f[:], 1.0)
                    e.memset(epsT[:], 1e-5)
                    e.memset(magicT[:], MAGIC)
                    e.memset(nmagicT[:], -MAGIC)
                    e.memset(SCAL[:], 0.0)
                    e.memset(srow[:], 0.0)
                    e.tensor_copy(out=bias["bi"][:], in_=braw["bi"][:])
                    e.tensor_copy(out=bias["b2"][:], in_=braw["b2"][:])
                    e.tensor_scalar(out=bias["bq4"][:], in0=braw["bq"][:], scalar1=0.25,
                                    scalar2=None, op0=OP.mult)
                    e.tensor_scalar(out=bias["bk4"][:], in0=braw["bk"][:], scalar1=0.25,
                                    scalar2=None, op0=OP.mult)
                    _l = None
                    for k in range(5):
                        for c in range(EC):
                            _l = e.tensor_scalar(out=w1k_bf[:, k, c, :], in0=w1_bf[:, 2 + c, :],
                                                 scalar1=cwT_s[:, c, k:k + 1], scalar2=None,
                                                 op0=OP.mult)
                    return _l
                vc = sch.emit("vector", vconsts, waits=[all_ld], inc="vS")

                def gmasks(e):
                    e.memset(MASK[:], 0.0)
                    e.affine_select(out=MASK[:], in_=MASK[:], compare_op=OP.is_ge,
                                    fill=NEG, base=0, pattern=[[1, 192]], channel_multiplier=-1)
                    e.affine_select(out=MASK[:], in_=MASK[:], compare_op=OP.is_ge,
                                    fill=NEG, base=LOOK - 1, pattern=[[-1, 192]], channel_multiplier=1)
                    e.tensor_copy(out=MASK0[:], in_=MASK[:])
                    return e.affine_select(out=MASK0[:], in_=MASK0[:], compare_op=OP.is_ge,
                                           fill=NEG, base=-LOOK, pattern=[[1, 192]],
                                           channel_multiplier=0)
                sch.emit("gpsimd", gmasks, inc="gS")

                t1 = sch.emit("tensor", lambda e: e.matmul(psI[:, 0:E], ones_f[:], bv1[:],
                                                           start=True, stop=True),
                              waits=[all_ld, ("vS", vc)], inc="tS")
                v1 = sch.emit("vector", lambda e: e.tensor_copy(out=BV[:], in_=psI[:, 0:E]),
                              waits=[("tS", t1)], inc="vS")
                t2 = sch.emit("tensor", lambda e: [
                    e.matmul(psI[:, nh * 512:(nh + 1) * 512], ones_f[:],
                             bo1[:, nh * 512:(nh + 1) * 512], start=True, stop=True)
                    for nh in range(2)][-1],
                              waits=[("vS", v1)], inc="tS")
                v2 = sch.emit("vector", lambda e: e.tensor_copy(out=BO[:], in_=psI[:]),
                              waits=[("tS", t2)], inc="vS")

                def t_b1e(e):
                    _l = None
                    for mh in range(EC):
                        for c in range(2):
                            e.matmul(psBE[:, mh, :], w1f[:, c, mh * 128:(mh + 1) * 128],
                                     braw["cb"][:, c, :], start=(c == 0), stop=False)
                        for c in range(2):
                            _l = e.matmul(psBE[:, mh, :], w1f[:, 2 + c, mh * 128:(mh + 1) * 128],
                                          braw["bp"][:, c, :], start=False, stop=(c == 1))
                    return _l
                t3 = sch.emit("tensor", t_b1e, waits=[("ld", lw1f), ("vS", vc)], inc="tS")
                sch.emit("vector", lambda e: e.tensor_tensor(out=bias["b1e"][:], in0=psBE[:],
                                                             in1=braw["b1"][:], op=OP.add),
                         waits=[("tS", t3)], inc="vS")
                sch.barrier()

            def bcast_scal(wait):
                t = sch.emit("tensor", lambda e: e.matmul(psBC[:], ones_f[:], srow[:],
                                                          start=True, stop=True),
                             waits=[wait], inc="tS")
                v = sch.emit("vector", lambda e: e.tensor_copy(out=SCAL[:], in_=psBC[:]),
                             waits=[("tS", t)], inc="vS")
                return ("vS", v)

            def collective(k, ready_wait):
                g1 = sch.emit("gpsimd", lambda e, k=k: e.dma_start(out=cc_in[k].ap(), in_=ccs[:]),
                              waits=[ready_wait], inc="gd", amt=16)
                sch.emit("gpsimd", lambda e, k=k: e.collective_compute(
                    "AllReduce", OP.max, replica_groups=[list(range(NCORES))],
                    ins=[cc_in[k].ap().opt()], outs=[cc_out[k].ap().opt()]),
                    waits=[("gd", g1)], inc="cc")
                g3 = sch.emit("gpsimd", lambda e, k=k: e.dma_start(out=ccs[:], in_=cc_out[k].ap()),
                              waits=[("cc", sch.val("cc"))], inc="gd", amt=16)
                return ("gd", g3)

            def fold_to_scalar(pm_ap, wait, dstcol):
                t = sch.emit("tensor", lambda e: e.transpose(psXP[:], pm_ap, ident32[:]),
                             waits=[wait], inc="tS")
                v = sch.emit("vector", lambda e, c=dstcol: e.tensor_reduce(
                    out=ccs[0:1, c:c + 1], in_=psXP[:], axis=AX, op=OP.max),
                    waits=[("tS", t)], inc="vS")
                return ("vS", v)

            cut(2)
            with contextlib.ExitStack() as esA:
                EA = esA.enter_context
                xfm_g = EA(nc.sbuf_tensor("xfm_g", [128, 2, DC, 512], BF16))
                h_t = EA(nc.sbuf_tensor("h_t", [128, EC, S + 4], F32))
                h_bf = EA(nc.sbuf_tensor("h_bf", [128, EC, S + 4], BF16))
                q4 = EA(nc.sbuf_tensor("q4", [128, EC, S], BF16))
                k4 = EA(nc.sbuf_tensor("k4", [128, EC, S], BF16))
                vtm = EA(nc.sbuf_tensor("vtm", [128, NT, E], BF16))
                pb_bf = EA(nc.sbuf_tensor("pb_bf", [128, EC, S], BF16))
                summ_bf = EA(nc.sbuf_tensor("summ_bf", [128, EC, S], BF16))
                u_bf = EA(nc.sbuf_tensor("u_bf", [128, EC, S], BF16))
                gate_bc = EA(nc.sbuf_tensor("gate_bc", [128, S], F32))
                gate_row = EA(nc.sbuf_tensor("gate_row", [1, 512], F32))
                xtmb = EA(nc.sbuf_tensor("xtmb", [128, 2, D], F32))
                xbf_t = EA(nc.sbuf_tensor("xbf_t", [128, 2, D], BF16))
                trbuf = EA(nc.sbuf_tensor("trbuf", [128, 1024], F32))
                ph_bf = EA(nc.sbuf_tensor("ph_bf", [128, EC, 128], BF16))
                pbc_bf = EA(nc.sbuf_tensor("pbc_bf", [128, E], BF16))
                scb = EA(nc.sbuf_tensor("scb", [128, NT, 192], F32))
                e0b = EA(nc.sbuf_tensor("e0b", [128, NT, 192], F32))
                t8b = EA(nc.sbuf_tensor("t8b", [128, NT, 8], F32))
                negmb = EA(nc.sbuf_tensor("negmb", [128, NT], F32))
                rsum = EA(nc.sbuf_tensor("rsum", [128, NT], F32))
                rinvb = EA(nc.sbuf_tensor("rinvb", [128, NT], F32))
                wT_bf = EA(nc.sbuf_tensor("wT_bf", [128, 2, 192], BF16))
                wtA = EA(nc.sbuf_tensor("wtA", [128, NT, 128], BF16))
                wtB = EA(nc.sbuf_tensor("wtB", [128, NT, 128], BF16))
                dtile = EA(nc.sbuf_tensor("dtile", [128, 2, 512], F32))
                ln1buf = EA(nc.sbuf_tensor("ln1buf", [128, 4, D], F32))
                bnb = EA(nc.sbuf_tensor("bnb", [128, 4, 2, 6], F32))
                mvb = EA(nc.sbuf_tensor("mvb", [128, 4, 2, 2], F32))
                lnsc = EA(nc.sbuf_tensor("lnsc", [128, 4, 6], F32))
                meanb = EA(nc.sbuf_tensor("meanb", [128, 4], F32))
                varb = EA(nc.sbuf_tensor("varb", [128, 4], F32))
                rstdb = EA(nc.sbuf_tensor("rstdb", [128, 4], F32))
                sxb = EA(nc.sbuf_tensor("sxb", [128, 32], F32))
                xres = xtmb

                for b in range(B):
                    if b == 1:
                        cut(35)
                    with contextlib.ExitStack() as es01:
                        E01 = es01.enter_context
                        psXT = [E01(nc.psum_tensor(un(f"psXT{i}"), [128, DC, 128], BF16))
                                for i in range(2)]
                        psH = E01(nc.psum_tensor(un("psH"), [128, EC, 512], F32))
                        xread_free = [None, None]
                        xbf_free = [None, None]
                        psxt_free = [None, None]
                        h0_free = [None]
                        sch.emit("vector", lambda e: [e.memset(h_t[:, :, 0:4], 0.0),
                                                      e.memset(h_bf[:, :, 0:4], 0.0)][-1], inc="vS")
                        xfmg_free = [None, None]
                        for g in range(4):
                            gp = g % 2
                            last_ev = None
                            for i in range(4):
                                tt = 4 * g + i
                                pid = tt % 2
                                lsem = "xA" if pid == 0 else "xB"
                                lv = sch.emit("sync", lambda e, b=b, tt=tt, pid=pid: e.dma_start(
                                    out=xtmb[:, pid, :], in_=d_x[b, tt * 128:(tt + 1) * 128, :]),
                                    waits=[xread_free[pid]], inc=lsem, amt=16)
                                cv = sch.emit("vector", lambda e, pid=pid: e.tensor_copy(
                                    out=xbf_t[:, pid, :], in_=xtmb[:, pid, :]),
                                    waits=[(lsem, lv), xbf_free[pid]], inc="vS")
                                tv = sch.emit("tensor", lambda e, pid=pid: [
                                    e.transpose(psXT[pid][:, c, :],
                                                xbf_t[:, pid, c * 128:(c + 1) * 128], ident_bf[:])
                                    for c in range(DC)][-1],
                                    waits=[("vS", cv), psxt_free[pid]], inc="tS")
                                ev = sch.emit("vector", lambda e, i=i, gp=gp, pid=pid: e.tensor_copy(
                                    out=xfm_g[:, gp, :, i * 128:(i + 1) * 128], in_=psXT[pid][:]),
                                    waits=[("tS", tv), xfmg_free[gp]], inc="vS")
                                xread_free[pid] = ("vS", cv)
                                xbf_free[pid] = ("tS", tv)
                                psxt_free[pid] = ("vS", ev)
                                last_ev = ev
                            tm = sch.emit("tensor", lambda e, gp=gp: [
                                e.matmul(psH[:, mh, :], wi_bf[:, c, mh * 128:(mh + 1) * 128],
                                         xfm_g[:, gp, c, :],
                                         start=(c == 0), stop=(c == DC - 1))
                                for mh in range(EC) for c in range(DC)][-1],
                                waits=[("vS", last_ev), h0_free[0]], inc="tS")
                            xfmg_free[gp] = ("tS", tm)
                            sv = sch.emit("scalar", lambda e, g=g: [
                                e.activation(out=h_t[:, mh, 4 + g * 512: 4 + (g + 1) * 512],
                                             in_=psH[:, mh, :], func=ACT.Identity,
                                             bias=bias["bi"][:, mh, :], scale=1.0)
                                for mh in range(EC)][-1],
                                waits=[("tS", tm)], inc="sS")
                            h0_free[0] = ("sS", sv)
                        sch.barrier()

                    cut(3)
                    for st in range(2):
                        hc = sch.emit("vector", lambda e: [
                            e.tensor_copy(out=h_bf[:, c, :], in_=h_t[:, c, :])
                            for c in range(EC)][-1], inc="vS")
                        cut(19)
                        with contextlib.ExitStack() as esp:
                            psPB = esp.enter_context(nc.psum_tensor(un("psPB"), [128, E], F32))
                            psPBE = esp.enter_context(nc.psum_tensor(un("psPBE"), [128, 512], F32))
                            tv = sch.emit("vector", lambda e: [x for c in range(EC) for x in [
                                e.tensor_tensor(out=trbuf[:, 0:1024], in0=h_t[:, c, 4:2052:2],
                                                in1=h_t[:, c, 5:2052:2], op=OP.add),
                                e.tensor_tensor(out=trbuf[:, 0:512], in0=trbuf[:, 0:1024:2],
                                                in1=trbuf[:, 1:1024:2], op=OP.add),
                                e.tensor_tensor(out=trbuf[:, 0:256], in0=trbuf[:, 0:512:2],
                                                in1=trbuf[:, 1:512:2], op=OP.add),
                                e.tensor_tensor(out=trbuf[:, 0:128], in0=trbuf[:, 0:256:2],
                                                in1=trbuf[:, 1:256:2], op=OP.add),
                                e.tensor_scalar(out=ph_bf[:, c, :], in0=trbuf[:, 0:128],
                                                scalar1=1.0 / 16.0, scalar2=None, op0=OP.mult)]][-1],
                                inc="vS")
                            tp = sch.emit("tensor", lambda e: [
                                e.matmul(psPB[:], ph_bf[:, c, :], wp_bf[:, c, :],
                                         start=(c == 0), stop=(c == EC - 1)) for c in range(EC)][-1],
                                waits=[("vS", tv)], inc="tS")
                            pv = sch.emit("vector", lambda e: e.tensor_copy(out=pbc_bf[:], in_=psPB[:]),
                                          waits=[("tS", tp)], inc="vS")
                            cut(20)
                            pbe_free = [None]
                            for mh in range(EC):
                                for ng in range(4):
                                    te = sch.emit("tensor", lambda e, mh=mh, ng=ng: e.matmul(
                                        psPBE[:], pbc_bf[:, mh * 128:(mh + 1) * 128],
                                        r16_bf[:, ng * 512:(ng + 1) * 512], start=True, stop=True),
                                        waits=[("vS", pv), pbe_free[0]], inc="tS")
                                    se = sch.emit("scalar", lambda e, mh=mh, ng=ng: e.copy(
                                        out=pb_bf[:, mh, ng * 512:(ng + 1) * 512], in_=psPBE[:]),
                                        waits=[("tS", te)], inc="sS")
                                    pbe_free[0] = ("sS", se)
                            sch.barrier()

                        cut(21)
                        with contextlib.ExitStack() as esq:
                            psQK = [esq.enter_context(nc.psum_tensor(un(f"psQK{i}"), [128, 512], F32))
                                    for i in range(2)]
                            psV = [esq.enter_context(nc.psum_tensor(un(f"psV{i}"), [128, E], F32))
                                   for i in range(2)]
                            qk_free = [None, None]
                            v_free = [None, None]
                            for (wt, dst, bnm) in ((wq_bf, q4, "bq4"), (wk_bf, k4, "bk4")):
                                for mh in range(EC):
                                    for ng in range(4):
                                        pid = (mh * 4 + ng) % 2
                                        tq = sch.emit("tensor",
                                                      lambda e, wt=wt, mh=mh, ng=ng, pid=pid: [
                                                          e.matmul(psQK[pid][:],
                                                                   wt[:, c, mh * 128:(mh + 1) * 128],
                                                                   h_bf[:, c, 4 + ng * 512:
                                                                        4 + (ng + 1) * 512],
                                                                   start=(c == 0), stop=(c == EC - 1))
                                                          for c in range(EC)][-1],
                                                      waits=[("vS", hc), qk_free[pid]], inc="tS")
                                        sq = sch.emit("scalar",
                                                      lambda e, dst=dst, mh=mh, ng=ng, pid=pid,
                                                      bnm=bnm: e.activation(
                                                          out=dst[:, mh, ng * 512:(ng + 1) * 512],
                                                          in_=psQK[pid][:], func=ACT.Identity,
                                                          bias=bias[bnm][:, mh, :], scale=0.25),
                                                      waits=[("tS", tq)], inc="sS")
                                        qk_free[pid] = ("sS", sq)
                            for tt in range(NT):
                                pid = tt % 2
                                tvv = sch.emit("tensor", lambda e, tt=tt, pid=pid: [
                                    e.matmul(psV[pid][:],
                                             h_bf[:, c, 4 + tt * 128: 4 + (tt + 1) * 128],
                                             wv_bf[:, c, :], start=(c == 0), stop=(c == EC - 1))
                                    for c in range(EC)][-1],
                                    waits=[("vS", hc), v_free[pid]], inc="tS")
                                vv = sch.emit("vector", lambda e, tt=tt, pid=pid: e.tensor_tensor(
                                    out=vtm[:, tt, :], in0=psV[pid][:], in1=BV[:], op=OP.add),
                                    waits=[("tS", tvv)], inc="vS")
                                v_free[pid] = ("vS", vv)
                            sch.barrier()

                        cut(22)
                        with contextlib.ExitStack() as ess:
                            psSC = [ess.enter_context(nc.psum_tensor(un(f"psSC{i}"), [128, 192], F32))
                                    for i in range(2)]
                            psWT = [ess.enter_context(nc.psum_tensor(un("psWT"), [128, 192], BF16))
                                    for _ in range(2)]
                            psSM = [ess.enter_context(nc.psum_tensor(un("psSM"), [128, EC, 128], F32))
                                    for _ in range(2)]
                            sc_free = [None, None]
                            wt_free = [None, None]
                            sm_free = [None, None]
                            for tt in range(NT):
                                pid = tt % 2
                                if tt == 0:
                                    mz = sch.emit("vector", lambda e, pid=pid: e.memset(
                                        psSC[pid][:, 0:64], 0.0), waits=[sc_free[pid]], inc="vS")
                                    tw = [("vS", mz)]
                                    fn = lambda e, pid=pid: [
                                        e.matmul(psSC[pid][:, 64:192], q4[:, c, 0:128],
                                                 k4[:, c, 0:128], start=(c == 0), stop=(c == EC - 1))
                                        for c in range(EC)][-1]
                                else:
                                    tw = [sc_free[pid]]
                                    fn = lambda e, tt=tt, pid=pid: [
                                        e.matmul(psSC[pid][:], q4[:, c, tt * 128:(tt + 1) * 128],
                                                 k4[:, c, tt * 128 - 64:tt * 128 + 128],
                                                 start=(c == 0), stop=(c == EC - 1))
                                        for c in range(EC)][-1]
                                tsc = sch.emit("tensor", fn, waits=tw, inc="tS")
                                vsc = sch.emit("vector", lambda e, tt=tt, pid=pid: e.tensor_tensor(
                                    out=scb[:, tt, :], in0=psSC[pid][:],
                                    in1=(MASK0 if tt == 0 else MASK)[:], op=OP.add),
                                    waits=[("tS", tsc)], inc="vS")
                                sc_free[pid] = ("vS", vsc)
                                sch.emit("vector", lambda e, tt=tt: e.max(
                                    out=t8b[:, tt, :], in_=scb[:, tt, :]), inc="vS")
                            sch.emit("vector", lambda e: e.drain(), inc="vS")
                            dr2 = sch.emit("vector", lambda e: e.tensor_scalar(
                                out=negmb[:], in0=t8b[:, :, 0], scalar1=-1.0, scalar2=None,
                                op0=OP.mult), inc="vS")
                            dr2 = sch.emit("vector", lambda e: e.drain(), inc="vS")
                            exps = []
                            for tt in range(NT):
                                sexp = sch.emit("scalar", lambda e, tt=tt: e.activation(
                                    out=e0b[:, tt, :], in_=scb[:, tt, :], func=ACT.Exp,
                                    bias=negmb[:, tt:tt + 1], scale=1.0),
                                    waits=[("vS", dr2)], inc="sS")
                                exps.append(sexp)
                            for tt in range(NT):
                                sch.emit("vector", lambda e, tt=tt: e.tensor_scalar(
                                    out=scb[:, tt, :], in0=scb[:, tt, :], scalar1=t8b[:, tt, 7:8],
                                    scalar2=None, op0=OP.is_ge),
                                    waits=[("sS", exps[tt])], inc="vS")
                            ee_ = sch.emit("vector", lambda e: e.tensor_tensor(
                                out=e0b[:].rearrange("p t j -> p (t j)"),
                                in0=e0b[:].rearrange("p t j -> p (t j)"),
                                in1=scb[:].rearrange("p t j -> p (t j)"), op=OP.mult), inc="vS")
                            sch.emit("vector", lambda e: e.memset(e0b[0:1, 0, :], 0.0), inc="vS")
                            sch.emit("vector", lambda e: e.tensor_reduce(
                                out=rsum[:, :, None], in_=e0b[:], axis=AX, op=OP.add), inc="vS")
                            sch.emit("vector", lambda e: e.drain(), inc="vS")
                            sch.emit("vector", lambda e: e.tensor_scalar(
                                out=rsum[:], in0=rsum[:], scalar1=1e-30, scalar2=None, op0=OP.max),
                                inc="vS")
                            sch.emit("vector", lambda e: e.drain(), inc="vS")
                            sch.emit("vector", lambda e: e.reciprocal(out=rinvb[:], in_=rsum[:]),
                                     inc="vS")
                            sch.emit("vector", lambda e: e.drain(), inc="vS")
                            wTf = [None, None]
                            for tt in range(NT):
                                wp_ = tt % 2
                                wv_ = sch.emit("vector", lambda e, tt=tt, wp_=wp_: e.tensor_scalar(
                                    out=wT_bf[:, wp_, :], in0=e0b[:, tt, :],
                                    scalar1=rinvb[:, tt:tt + 1], scalar2=None, op0=OP.mult),
                                    waits=[wTf[wp_]], inc="vS")
                                twt = sch.emit("tensor", lambda e, wp_=wp_: [
                                    e.transpose(psWT[wp_][64:128, 0:128], wT_bf[:, wp_, 0:64],
                                                ident_bf[:]),
                                    e.transpose(psWT[wp_][:, 64:192], wT_bf[:, wp_, 64:192],
                                                ident_bf[:])][-1],
                                    waits=[("vS", wv_), wt_free[wp_]], inc="tS")
                                wTf[wp_] = ("tS", twt)
                                vwt = sch.emit("vector", lambda e, tt=tt, wp_=wp_: [
                                    e.tensor_copy(out=wtA[64:128, tt, :],
                                                  in_=psWT[wp_][64:128, 0:128]),
                                    e.tensor_copy(out=wtB[:, tt, :], in_=psWT[wp_][:, 64:192])][-1],
                                    waits=[("tS", twt)], inc="vS")
                                wt_free[wp_] = ("vS", vwt)
                                tsm = sch.emit("tensor", lambda e, tt=tt, wp_=wp_: [
                                    x for mh in range(EC) for x in
                                    ([e.matmul(psSM[wp_][:, mh, :],
                                               vtm[64:128, tt - 1, mh * 128:(mh + 1) * 128],
                                               wtA[64:128, tt, :], start=True, stop=False)]
                                     if tt > 0 else []) +
                                    [e.matmul(psSM[wp_][:, mh, :],
                                              vtm[:, tt, mh * 128:(mh + 1) * 128],
                                              wtB[:, tt, :], start=(tt == 0), stop=True)]][-1],
                                    waits=[("vS", vwt), sm_free[wp_]], inc="tS")
                                vsm = sch.emit("vector", lambda e, tt=tt, wp_=wp_: e.tensor_copy(
                                    out=summ_bf[:, :, tt * 128:(tt + 1) * 128], in_=psSM[wp_][:]),
                                    waits=[("tS", tsm)], inc="vS")
                                sm_free[wp_] = ("vS", vsm)
                            sch.barrier()

                        cut(23)
                        with contextlib.ExitStack() as esg:
                            psG = esg.enter_context(nc.psum_tensor(un("psG"), [1, 512], F32))
                            psGB = [esg.enter_context(nc.psum_tensor(un(f"psGB{i}"), [128, 512], F32))
                                    for i in range(2)]
                            g_free = [None]
                            grow_free = [None]
                            gb_free = [None, None]
                            for ng in range(4):
                                tg = sch.emit("tensor", lambda e, ng=ng: [
                                    e.matmul(psG[:], wh2_bf[:, c, :],
                                             h_bf[:, c, 4 + ng * 512: 4 + (ng + 1) * 512],
                                             start=(c == 0), stop=(c == EC - 1))
                                    for c in range(EC)][-1],
                                    waits=[g_free[0]], inc="tS")
                                sg = sch.emit("scalar", lambda e, ng=ng: e.activation(
                                    out=gate_row[:], in_=psG[:],
                                    func=ACT.Sigmoid, bias=bh2_s[:], scale=1.0),
                                    waits=[("tS", tg), grow_free[0]], inc="sS")
                                g_free[0] = ("sS", sg)
                                tgb = sch.emit("tensor", lambda e, ng=ng: e.matmul(
                                    psGB[ng % 2][:], ones_f[:], gate_row[:],
                                    start=True, stop=True),
                                    waits=[("sS", sg), gb_free[ng % 2]], inc="tS")
                                vgb = sch.emit("vector", lambda e, ng=ng: e.tensor_copy(
                                    out=gate_bc[:, ng * 512:(ng + 1) * 512], in_=psGB[ng % 2][:]),
                                    waits=[("tS", tgb)], inc="vS")
                                gb_free[ng % 2] = ("vS", vgb)
                                grow_free[0] = ("tS", tgb)
                            sch.barrier()

                        cut(24)
                        with contextlib.ExitStack() as esd:
                            psD1 = [esd.enter_context(nc.psum_tensor(un(f"psD1{i}"), [128, 512], F32))
                                    for i in range(2)]
                            psD2 = [esd.enter_context(nc.psum_tensor(un(f"psD2{i}"), [128, 512], F32))
                                    for i in range(2)]
                            d1_free = [None, None]
                            for mh in range(EC):
                                for ng in range(4):
                                    pid = (mh * 4 + ng) % 2
                                    def w1mm(e, mh=mh, ng=ng, pid=pid):
                                        sl = slice(4 + ng * 512, 4 + (ng + 1) * 512)
                                        for c in range(EC):
                                            e.matmul(psD1[pid][:],
                                                     w1_bf[:, c, mh * 128:(mh + 1) * 128],
                                                     h_bf[:, c, sl], start=(c == 0), stop=False)
                                        for k in range(5):
                                            for c in range(EC):
                                                e.matmul(psD1[pid][:],
                                                         w1k_bf[:, k, c, mh * 128:(mh + 1) * 128],
                                                         h_bf[:, c, ng * 512 + k:
                                                              ng * 512 + k + 512],
                                                         start=False, stop=False)
                                        for c in range(EC):
                                            e.matmul(psD1[pid][:],
                                                     w1_bf[:, 4 + c, mh * 128:(mh + 1) * 128],
                                                     pb_bf[:, c, ng * 512:(ng + 1) * 512],
                                                     start=False, stop=False)
                                        _l = None
                                        for c in range(EC):
                                            _l = e.matmul(psD1[pid][:],
                                                          w1_bf[:, 6 + c,
                                                                mh * 128:(mh + 1) * 128],
                                                          summ_bf[:, c,
                                                                  ng * 512:(ng + 1) * 512],
                                                          start=False, stop=(c == EC - 1))
                                        return _l
                                    t1m = sch.emit("tensor", w1mm, waits=[d1_free[pid]], inc="tS")
                                    s1m = sch.emit("scalar", lambda e, mh=mh, ng=ng, pid=pid:
                                                   e.activation(
                                                       out=u_bf[:, mh, ng * 512:(ng + 1) * 512],
                                                       in_=psD1[pid][:], func=ACT.Tanh,
                                                       bias=bias["b1e"][:, mh, :], scale=1.0),
                                                   waits=[("tS", t1m)], inc="sS")
                                    d1_free[pid] = ("sS", s1m)
                            cut(24.3)
                            udone = ("sS", sch.val("sS"))
                            d2_free = [None, None]
                            dtile_free = [None, None]
                            for mh in range(EC):
                                for ng in range(4):
                                    pid = (mh * 4 + ng) % 2
                                    t2m = sch.emit("tensor", lambda e, mh=mh, ng=ng, pid=pid: [
                                        e.matmul(psD2[pid][:], w2_bf[:, c, mh * 128:(mh + 1) * 128],
                                                 u_bf[:, c, ng * 512:(ng + 1) * 512],
                                                 start=(c == 0), stop=(c == EC - 1))
                                        for c in range(EC)][-1],
                                        waits=[udone, d2_free[pid]], inc="tS")
                                    s2m = sch.emit("scalar", lambda e, mh=mh, pid=pid: e.activation(
                                        out=dtile[:, pid, :], in_=psD2[pid][:], func=ACT.Identity,
                                        bias=bias["b2"][:, mh, :], scale=1.0),
                                        waits=[("tS", t2m), dtile_free[pid]], inc="sS")
                                    v2m = sch.emit("vector", lambda e, ng=ng, pid=pid: e.tensor_tensor(
                                        out=dtile[:, pid, :], in0=dtile[:, pid, :],
                                        in1=gate_bc[:, ng * 512:(ng + 1) * 512], op=OP.mult),
                                        waits=[("sS", s2m)], inc="vS")
                                    v3m = sch.emit("vector", lambda e, mh=mh, ng=ng, pid=pid:
                                                   e.tensor_tensor(
                                        out=h_t[:, mh, 4 + ng * 512: 4 + (ng + 1) * 512],
                                        in0=h_t[:, mh, 4 + ng * 512: 4 + (ng + 1) * 512],
                                        in1=dtile[:, pid, :], op=OP.add), inc="vS")
                                    d2_free[pid] = ("sS", s2m)
                                    dtile_free[pid] = ("vS", v3m)
                            sch.barrier()
                    cut(25)
                    hcf = sch.emit("vector", lambda e: [
                        e.tensor_copy(out=h_bf[:, c, :], in_=h_t[:, c, :])
                        for c in range(EC)][-1], inc="vS")
                    if dbg:
                        sch.emit("gpsimd", lambda e, b=b: e.dma_start(
                            out=d_dbg_h[b], in_=h_t[:, :, 4:2052]), waits=[("vS", hcf)],
                            inc="gd", amt=16)
                    with contextlib.ExitStack() as es3:
                        psAO = [es3.enter_context(nc.psum_tensor(un(f"psAO{i}"), [128, 2, 512], F32))
                                for i in range(2)]
                        ao_free = [None, None]
                        xr_free = [None, None]
                        for hb in range(4):
                            for ti in range(4):
                                tt = hb * 4 + ti
                                tt32 = b * 16 + tt
                                pid = tt % 2
                                lsem = "wA" if pid == 0 else "wB"
                                lx = sch.emit("sync", lambda e, b=b, tt=tt, pid=pid: e.dma_start(
                                    out=xres[:, pid, :], in_=d_x[b, tt * 128:(tt + 1) * 128, :]),
                                    waits=[xr_free[pid]], inc=lsem, amt=16)
                                tao = sch.emit("tensor", lambda e, tt=tt, pid=pid: [
                                    e.matmul(psAO[pid][:, nh, :],
                                             h_bf[:, c, 4 + tt * 128: 4 + (tt + 1) * 128],
                                             wo_bf[:, c, nh * 512:(nh + 1) * 512],
                                             start=(c == 0), stop=(c == EC - 1))
                                    for nh in range(2) for c in range(EC)][-1],
                                    waits=[("vS", hcf), ao_free[pid]], inc="tS")
                                vao = sch.emit("vector", lambda e, ti=ti, pid=pid: [
                                    e.scalar_tensor_tensor(
                                        out=ln1buf[:, ti, nh * 512:(nh + 1) * 512],
                                        in0=xres[:, pid, nh * 512:(nh + 1) * 512], scalar=2.0,
                                        in1=psAO[pid][:, nh, :], op0=OP.mult, op1=OP.add)
                                    for nh in range(2)][-1],
                                    waits=[(lsem, lx), ("tS", tao), ("gd", sch.val("gd"))],
                                    inc="vS")
                                sch.emit("vector", lambda e, ti=ti: e.tensor_tensor(
                                    out=ln1buf[:, ti, :], in0=ln1buf[:, ti, :], in1=BO[:],
                                    op=OP.add), inc="vS")
                                ao_free[pid] = ("vS", vao)
                                xr_free[pid] = ("vS", vao)
                                sch.emit("vector", lambda e, ti=ti: [
                                    e.bn_stats(out=bnb[:, ti, 0, :], in_=ln1buf[:, ti, 0:512]),
                                    e.bn_stats(out=bnb[:, ti, 1, :], in_=ln1buf[:, ti, 512:1024])][-1],
                                    inc="vS")
                            def combine(e):
                                for ti in range(4):
                                    e.bn_aggr(out=mvb[:, ti, 0, :], in_=bnb[:, ti, 0, :])
                                    e.bn_aggr(out=mvb[:, ti, 1, :], in_=bnb[:, ti, 1, :])
                                e.drain()
                                e.tensor_tensor(out=lnsc[:, :, 0], in0=mvb[:, :, 0, 0],
                                                in1=mvb[:, :, 1, 0], op=OP.subtract)
                                e.tensor_tensor(out=lnsc[:, :, 2], in0=mvb[:, :, 0, 0],
                                                in1=mvb[:, :, 1, 0], op=OP.add)
                                e.tensor_tensor(out=lnsc[:, :, 3], in0=mvb[:, :, 0, 1],
                                                in1=mvb[:, :, 1, 1], op=OP.add)
                                e.drain()
                                e.tensor_scalar(out=lnsc[:, :, 1], in0=lnsc[:, :, 0], scalar1=0.5,
                                                scalar2=None, op0=OP.mult)
                                e.tensor_scalar(out=meanb[:], in0=lnsc[:, :, 2], scalar1=0.5,
                                                scalar2=None, op0=OP.mult)
                                e.drain()
                                e.tensor_tensor(out=lnsc[:, :, 4], in0=lnsc[:, :, 1],
                                                in1=lnsc[:, :, 1], op=OP.mult)
                                e.drain()
                                return e.scalar_tensor_tensor(
                                    out=varb[:], in0=lnsc[:, :, 3], scalar=0.5,
                                    in1=lnsc[:, :, 4], op0=OP.mult, op1=OP.add)
                            cv = sch.emit("vector", combine, inc="vS")
                            sqv = sch.emit("scalar", lambda e: e.activation(
                                out=varb[:], in_=varb[:], func=ACT.Sqrt, bias=epsT[:], scale=1.0),
                                waits=[("vS", cv)], inc="sS")
                            sch.emit("vector", lambda e: e.reciprocal(out=rstdb[:], in_=varb[:]),
                                     waits=[("sS", sqv)], inc="vS")
                            drv = sch.emit("vector", lambda e: e.drain(), inc="vS")
                            for ti in range(4):
                                tt32 = b * 16 + hb * 4 + ti
                                nv = sch.emit("vector", lambda e, ti=ti: e.tensor_scalar(
                                    out=ln1buf[:, ti, :], in0=ln1buf[:, ti, :],
                                    scalar1=meanb[:, ti:ti + 1], scalar2=rstdb[:, ti:ti + 1],
                                    op0=OP.subtract, op1=OP.mult), waits=[("vS", drv)], inc="vS")
                                sch.emit("vector", lambda e, ti=ti, tt32=tt32: e.tensor_reduce(
                                    out=sxb[:, tt32:tt32 + 1], in_=ln1buf[:, ti, :], axis=AX,
                                    op=OP.max, apply_absolute_value=True), inc="vS")
                                sch.emit("gpsimd", lambda e, ti=ti, tt32=tt32: e.dma_start(
                                    out=d_ln1[tt32], in_=ln1buf[:, ti, :]),
                                    waits=[("vS", sch.val("vS"))], inc="gd", amt=16)
                                if dbg:
                                    sch.emit("gpsimd", lambda e, ti=ti, tt32=tt32: e.dma_start(
                                        out=d_dbg_ln1[tt32], in_=ln1buf[:, ti, :]),
                                        waits=[("vS", sch.val("vS"))], inc="gd", amt=16)
                        sch.barrier()
                ln1_sd = ("gd", sch.val("gd"))

                cut(40)
                f1 = sch.emit("vector", lambda e: e.tensor_reduce(
                    out=foldt[:], in_=sxb[:], axis=AX, op=OP.max), inc="vS")
                fw = fold_to_scalar(foldt[:], ("vS", f1), 0)
                b1w = collective(0, fw)
                def d1(e):
                    e.tensor_scalar(out=srow[:, 1:2], in0=ccs[0:1, 0:1], scalar1=1.0 / 32767.0,
                                    scalar2=1e-12, op0=OP.mult, op1=OP.max)
                    e.drain()
                    e.reciprocal(out=srow[:, 0:1], in_=srow[:, 1:2])
                    return e.drain()
                dv1 = sch.emit("vector", d1, waits=[b1w], inc="vS")
                bc1 = bcast_scal(("vS", dv1))
                sch.barrier()

            gr_sp_done = [None] * ICN
            gp_sp_done = [None] * ICN
            wub_sp_done = [None] * ICN
            cut(45)
            with contextlib.ExitStack() as esF:
                EF = esF.enter_context
                xr_fm = EF(nc.sbuf_tensor("xr_fm", [128, DC, TPC], BF16))
                lbuf = EF(nc.sbuf_tensor("lbuf", [128, 2, D], F32))
                xrb_t = EF(nc.sbuf_tensor("xrb_t", [128, 2, D], BF16))
                wraw = EF(nc.sbuf_tensor("wraw", [128, 2, D], F32))
                bmax = EF(nc.sbuf_tensor("bmax", [128, 2, 32], F32))
                binv = EF(nc.sbuf_tensor("binv", [128, 2, 32], F32))
                wq1 = EF(nc.sbuf_tensor("wq1", [128, D], F32))
                wqb = EF(nc.sbuf_tensor("wqb", [128, 2, D], BF16))
                wt_bf = EF(nc.sbuf_tensor("wt_bf", [128, 2, DC, 128], BF16))
                wt_g = EF(nc.sbuf_tensor("wt_g", [128, 2, DC, 128], BF16))
                wt_u = EF(nc.sbuf_tensor("wt_u", [128, 2, DC, 128], BF16))
                gmb = EF(nc.sbuf_tensor("gmb", [128, 224], F32))
                umb = EF(nc.sbuf_tensor("umb", [128, 224], F32))
                gpmb = EF(nc.sbuf_tensor("gpmb", [128, 64], F32))
                opmb = EF(nc.sbuf_tensor("opmb", [128, 64], F32))
                HW = TPC // 2
                gbuf = EF(nc.sbuf_tensor("gbuf", [128, 2, HW], F32))
                grb2 = EF(nc.sbuf_tensor("grb2", [128, 2, HW], F32))
                sgb2 = EF(nc.sbuf_tensor("sgb2", [128, 2, HW], F32))
                gpb = EF(nc.sbuf_tensor("gpb", [128, 2, HW], F32))
                psWT2 = [EF(nc.psum_tensor(un(f"psWT2{i}"), [128, DC, 128], BF16)) for i in range(2)]
                psUG = [EF(nc.psum_tensor(un(f"psUG{i}"), [128, 512], F32)) for i in range(4)]

                psXR = psWT2
                lb_free = [None, None]
                xrps_free = [None, None]
                xrb_free = [None, None]
                for t32 in range(NT32):
                    pid = t32 % 2
                    lsem = "xA" if pid == 0 else "xB"
                    lv = sch.emit("sync", lambda e, t32=t32, pid=pid: e.dma_start(
                        out=lbuf[:, pid, :], in_=d_ln1[t32]),
                        waits=[lb_free[pid], ln1_sd], inc=lsem, amt=16)
                    va = sch.emit("vector", lambda e, pid=pid: e.tensor_scalar(
                        out=lbuf[:, pid, :], in0=lbuf[:, pid, :], scalar1=SCAL[:, 0:1],
                        scalar2=MAGIC, op0=OP.mult, op1=OP.add),
                        waits=[(lsem, lv), bc1], inc="vS")
                    vb = sch.emit("vector", lambda e, pid=pid: [
                        e.drain(),
                        e.tensor_scalar(
                            out=xrb_t[:, pid, :], in0=lbuf[:, pid, :], scalar1=MAGIC,
                            scalar2=None, op0=OP.subtract)][-1],
                        waits=[xrb_free[pid]], inc="vS")
                    lb_free[pid] = ("vS", vb)
                    tx = sch.emit("tensor", lambda e, pid=pid: [
                        e.transpose(psXR[pid][:, c, :], xrb_t[:, pid, c * 128:(c + 1) * 128],
                                    ident_bf[:]) for c in range(DC)][-1],
                        waits=[("vS", vb), xrps_free[pid]], inc="tS")
                    xrb_free[pid] = ("tS", tx)
                    vx = sch.emit("vector", lambda e, t32=t32, pid=pid: e.tensor_copy(
                        out=xr_fm[:, :, t32 * 128:(t32 + 1) * 128], in_=psXR[pid][:]),
                        waits=[("tS", tx)], inc="vS")
                    xrps_free[pid] = ("vS", vx)
                xr_done = ("vS", sch.val("vS"))
                cut(46)
                if dbg:
                    sch.emit("gpsimd", lambda e: e.dma_start(
                        out=d_dbg_xr[0], in_=xr_fm[:, :, 3584:4096]),
                        waits=[xr_done], inc="gd", amt=16)

                wraw_free = [None, None]
                wq1_free = [None]
                wtp_free = list(xrps_free)

                wqbT_free = [None, None]

                def wqdq(dsrc, pid, wslot):
                    lsem = "wA" if pid == 0 else "wB"
                    lv = sch.emit("sync", lambda e, d=dsrc, pid=pid: e.dma_start(
                        out=wraw[:, pid, :], in_=d), waits=[wraw_free[pid]], inc=lsem, amt=16)
                    def vq(e, pid=pid):
                        e.tensor_reduce(out=bmax[:, pid, :],
                                        in_=wraw[:, pid, :].rearrange("p (b k) -> p b k", k=32),
                                        axis=AX, op=OP.max, apply_absolute_value=True)
                        e.drain()
                        e.tensor_scalar(out=bmax[:, pid, :], in0=bmax[:, pid, :],
                                        scalar1=1.0 / 7.0, scalar2=1e-12, op0=OP.mult, op1=OP.max)
                        e.drain()
                        e.reciprocal(out=binv[:, pid, :], in_=bmax[:, pid, :])
                        e.drain()
                        return e.tensor_tensor(
                            out=wq1[:].rearrange("p (b k) -> p b k", k=32),
                            in0=wraw[:, pid, :].rearrange("p (b k) -> p b k", k=32),
                            in1=binv[:, pid, :, None].broadcast_to((128, 32, 32)), op=OP.mult)
                    v1 = sch.emit("vector", vq, waits=[(lsem, lv), wq1_free[0]], inc="vS")
                    s1 = sch.emit("scalar", lambda e: e.activation(
                        out=wq1[:], in_=wq1[:], func=ACT.Identity, bias=magicT[:], scale=1.0),
                        waits=[("vS", v1)], inc="sS")
                    s2 = sch.emit("scalar", lambda e: e.activation(
                        out=wq1[:], in_=wq1[:], func=ACT.Identity, bias=nmagicT[:], scale=1.0),
                        inc="sS")
                    v2 = sch.emit("vector", lambda e, pid=pid, wslot=wslot: e.tensor_tensor(
                        out=wqb[:, wslot, :].rearrange("p (b k) -> p b k", k=32),
                        in0=wq1[:].rearrange("p (b k) -> p b k", k=32),
                        in1=bmax[:, pid, :, None].broadcast_to((128, 32, 32)), op=OP.mult),
                        waits=[("sS", s2), wqbT_free[wslot]], inc="vS")
                    wraw_free[pid] = ("vS", v2)
                    wq1_free[0] = ("vS", v2)
                    return ("vS", v2)

                def wtransp(pid, wq_w):
                    tv = sch.emit("tensor", lambda e, pid=pid: [
                        e.transpose(psWT2[pid][:, c, :], wqb[:, pid, c * 128:(c + 1) * 128],
                                    ident_bf[:]) for c in range(DC)][-1],
                        waits=[wq_w, wtp_free[pid]], inc="tS")
                    vv = sch.emit("vector", lambda e, pid=pid: e.tensor_copy(
                        out=wt_bf[:, pid, :, :], in_=psWT2[pid][:]),
                        waits=[("tS", tv)], inc="vS")
                    wtp_free[pid] = ("vS", vv)
                    return ("vS", vv)

                ug_free = [None] * 4
                gsp_free = [None, None]
                wtg_free = [None, None]
                wtu_free = [None, None]
                prep_mark = {}

                wqv_mark = {}

                def prep_VS(ic):
                    pid = ic % 2
                    wqv_mark[ic] = {
                        "g": wqdq(dw["wgate"][ic], pid, 0),
                        "u": wqdq(dw["wup"][ic], pid, 1)}

                def prep_T(ic):
                    pid = ic % 2
                    marks = {}
                    for wname, wslot, dst, dfree in (("g", 0, wt_g, wtg_free),
                                                     ("u", 1, wt_u, wtu_free)):
                        tv = sch.emit("tensor", lambda e, pid=pid, wslot=wslot: [
                            e.transpose(psWT2[pid][:, c, :], wqb[:, wslot, c * 128:(c + 1) * 128],
                                        ident_bf[:]) for c in range(DC)][-1],
                            waits=[wqv_mark[ic][wname], wtp_free[pid]], inc="tS")
                        wqbT_free[wslot] = ("tS", tv)
                        vv = sch.emit("vector", lambda e, pid=pid, dst=dst: e.tensor_copy(
                            out=dst[:, pid, :, :], in_=psWT2[pid][:]),
                            waits=[("tS", tv), dfree[pid]], inc="vS")
                        wtp_free[pid] = ("vS", vv)
                        marks[wname] = ("vS", vv)
                    su = sch.emit("gpsimd", lambda e, ic=ic, pid=pid: e.dma_start(
                        out=d_wub[ic], in_=wt_u[:, pid, :, :].rearrange("p c m -> p (c m)")),
                        waits=[marks["u"]], inc="gd", amt=16)
                    wtu_free[pid] = ("gd", su)
                    wub_sp_done[ic] = ("gd", su)
                    prep_mark[ic] = marks

                prep_VS(0)
                prep_T(0)
                prep_VS(1)
                prep_T(1)
                cut(47)
                for ic in range(ICN):
                    pid = ic % 2
                    if ic == 4:
                        cut(48)
                    if ic == 12:
                        cut(48.5)
                    if ic + 2 < ICN:
                        prep_VS(ic + 2)
                    last_tm = None
                    sp = None
                    for tc in range(8):
                        bpid = tc % 4
                        tm = sch.emit("tensor", lambda e, tc=tc, pid=pid, bpid=bpid: [
                            e.matmul(psUG[bpid][:], wt_g[:, pid, c, :],
                                     xr_fm[:, c, tc * 512:(tc + 1) * 512],
                                     start=(c == 0), stop=(c == DC - 1)) for c in range(DC)][-1],
                            waits=[prep_mark[ic]["g"], xr_done] + (ug_free[bpid] or []),
                            inc="tS")
                        sv = sch.emit("scalar", lambda e, tc=tc, pid=pid, bpid=bpid: e.copy(
                            out=gpb[:, pid, (tc % 4) * 512:((tc % 4) + 1) * 512],
                            in_=psUG[bpid][:]),
                            waits=[("tS", tm), gsp_free[pid] if tc % 4 == 0 else None],
                            inc="sS")
                        vr = sch.emit("vector", lambda e, ic=ic, tc=tc, pid=pid: e.tensor_reduce(
                            out=gmb[:, ic * 8 + tc: ic * 8 + tc + 1],
                            in_=gpb[:, pid, (tc % 4) * 512:((tc % 4) + 1) * 512],
                            axis=AX, op=OP.max, apply_absolute_value=True),
                            waits=[("sS", sv)], inc="vS")
                        ug_free[bpid] = [("sS", sv)]
                        last_tm = tm
                        if tc % 4 == 3:
                            hf = tc // 4
                            sp = sch.emit("gpsimd", lambda e, ic=ic, hf=hf, pid=pid: e.dma_start(
                                out=d_gr[ic, :, hf * 2048:(hf + 1) * 2048], in_=gpb[:, pid, :]),
                                waits=[("sS", sv), ("vS", vr)], inc="gd", amt=16)
                            gsp_free[pid] = ("gd", sp)
                    gr_sp_done[ic] = ("gd", sp)
                    wtg_free[pid] = ("tS", last_tm)
                    if ic + 2 < ICN:
                        prep_T(ic + 2)

                cut(49)
                fg = sch.emit("vector", lambda e: e.tensor_reduce(
                    out=foldt[:], in_=gmb[:, 0:ICN * 8], axis=AX, op=OP.max), inc="vS")
                fw2 = fold_to_scalar(foldt[:], ("vS", fg), 0)
                b2w = collective(1, fw2)
                def d2(e):
                    e.tensor_scalar(out=srow[:, 3:4], in0=ccs[0:1, 0:1], scalar1=srow[:, 1:2],
                                    scalar2=1.0 / 32767.0, op0=OP.mult, op1=OP.mult)
                    e.drain()
                    e.tensor_scalar(out=srow[:, 3:4], in0=srow[:, 3:4], scalar1=1e-12,
                                    scalar2=None, op0=OP.max)
                    e.drain()
                    e.reciprocal(out=srow[:, 2:3], in_=srow[:, 3:4])
                    e.drain()
                    e.tensor_scalar(out=srow[:, 2:3], in0=srow[:, 2:3], scalar1=srow[:, 1:2],
                                    scalar2=None, op0=OP.mult)
                    return e.drain()
                dv2 = sch.emit("vector", d2, waits=[b2w], inc="vS")
                bc2 = bcast_scal(("vS", dv2))
                if dbg:
                    sch.emit("gpsimd", lambda e: e.dma_start(out=d_dbg_gum[0], in_=gmb[:, 0:192]),
                             waits=[("vS", sch.val("vS"))], inc="gd", amt=16)
                    sch.emit("gpsimd", lambda e: e.dma_start(
                        out=d_dbg_xr[2], in_=xr_fm[:, :, 3584:4096]),
                        waits=[("vS", sch.val("vS"))], inc="gd", amt=16)
                    sch.emit("gpsimd", lambda e: e.dma_start(out=d_dbg_s2, in_=srow[:]),
                             waits=[("vS", sch.val("vS"))], inc="gd", amt=16)
                sch.barrier()

                cut(50)
                wub_free = [("sd", sch.val("sd")), ("sd", sch.val("sd"))]
                gb_free = [None, None]
                gp_free = [("sd", sch.val("sd")), ("sd", sch.val("sd"))]
                chain_free = [None, None]
                for ic in range(ICN):
                    pid = ic % 2
                    wsem = "wA" if pid == 0 else "wB"
                    lw = sch.emit("sync", lambda e, ic=ic, pid=pid: e.dma_start(
                        out=wt_bf[:, pid, :, :].rearrange("p c m -> p (c m)"), in_=d_wub[ic]),
                        waits=[wub_free[pid], wub_sp_done[ic]], inc=wsem, amt=16)
                    for tc in range(8):
                        bpid = tc % 4
                        tm = sch.emit("tensor", lambda e, tc=tc, pid=pid, bpid=bpid: [
                            e.matmul(psUG[bpid][:], wt_bf[:, pid, c, :],
                                     xr_fm[:, c, tc * 512:(tc + 1) * 512],
                                     start=(c == 0), stop=(c == DC - 1)) for c in range(DC)][-1],
                            waits=[(wsem, lw)] + (ug_free[bpid] if isinstance(ug_free[bpid], list)
                                                  else [ug_free[bpid]]), inc="tS")
                        vr = sch.emit("vector", lambda e, ic=ic, tc=tc, bpid=bpid: e.tensor_reduce(
                            out=umb[:, ic * 8 + tc: ic * 8 + tc + 1], in_=psUG[bpid][:],
                            axis=AX, op=OP.max, apply_absolute_value=True),
                            waits=[("tS", tm)], inc="vS")
                        ug_free[bpid] = [("vS", vr)]
                    wub_free[pid] = ("tS", sch.val("tS"))
                    sgp = None
                    for hf in range(2):
                        hpid = (2 * ic + hf) % 2
                        gsem = "xA" if hpid == 0 else "xB"
                        lg = sch.emit("sync", lambda e, ic=ic, hf=hf, hpid=hpid: e.dma_start(
                            out=gbuf[:, hpid, :], in_=d_gr[ic, :, hf * 2048:(hf + 1) * 2048]),
                            waits=[gb_free[hpid], gr_sp_done[ic]], inc=gsem, amt=16)
                        sa = sch.emit("scalar", lambda e, hpid=hpid: e.activation(
                            out=grb2[:, hpid, :], in_=gbuf[:, hpid, :], func=ACT.Identity,
                            bias=magicT[:], scale=SCAL[:, 2:3]),
                            waits=[(gsem, lg), bc2, chain_free[hpid]], inc="sS")
                        gb_free[hpid] = ("sS", sa)
                        sgr = sch.emit("scalar", lambda e, hpid=hpid: e.activation(
                            out=grb2[:, hpid, :], in_=grb2[:, hpid, :], func=ACT.Identity,
                            bias=nmagicT[:], scale=1.0), inc="sS")
                        ssig = sch.emit("scalar", lambda e, hpid=hpid: e.activation(
                            out=sgb2[:, hpid, :], in_=grb2[:, hpid, :], func=ACT.Sigmoid,
                            bias=0.0, scale=SCAL[:, 3:4]), inc="sS")
                        sb2 = sch.emit("scalar", lambda e, hpid=hpid: e.activation(
                            out=sgb2[:, hpid, :], in_=sgb2[:, hpid, :], func=ACT.Identity,
                            bias=magicT[:], scale=65536.0), inc="sS")
                        vsr = sch.emit("vector", lambda e, hpid=hpid: e.tensor_scalar(
                            out=sgb2[:, hpid, :], in0=sgb2[:, hpid, :], scalar1=MAGIC,
                            scalar2=65535.0, op0=OP.subtract, op1=OP.min),
                            waits=[("sS", sb2)], inc="vS")
                        ggp = sch.emit("gpsimd", lambda e, hpid=hpid: e.tensor_tensor(
                            out=gpb[:, hpid, :], in0=grb2[:, hpid, :], in1=sgb2[:, hpid, :],
                            op=OP.mult),
                            waits=[("vS", vsr), gp_free[hpid]], inc="gS")
                        chain_free[hpid] = ("gS", ggp)
                        vgm = sch.emit("vector", lambda e, ic=ic, hf=hf, hpid=hpid: e.tensor_reduce(
                            out=gpmb[:, 2 * ic + hf: 2 * ic + hf + 1], in_=gpb[:, hpid, :],
                            axis=AX, op=OP.max, apply_absolute_value=True),
                            waits=[("gS", ggp)], inc="vS")
                        sgp = sch.emit("gpsimd", lambda e, ic=ic, hf=hf, hpid=hpid: e.dma_start(
                            out=d_gp[ic, :, hf * 2048:(hf + 1) * 2048], in_=gpb[:, hpid, :]),
                            waits=[("vS", vgm)], inc="gd", amt=16)
                        gp_free[hpid] = ("gd", sgp)
                    gp_sp_done[ic] = ("gd", sgp)

                if dbg:
                    sch.emit("gpsimd", lambda e: e.dma_start(out=d_dbg_gum[1], in_=umb[:, 0:192]),
                             waits=[("vS", sch.val("vS"))], inc="gd", amt=16)
                fu = sch.emit("vector", lambda e: e.tensor_reduce(
                    out=foldt[:], in_=umb[:, 0:ICN * 8], axis=AX, op=OP.max), inc="vS")
                fw3a = fold_to_scalar(foldt[:], ("vS", fu), 0)
                fp = sch.emit("vector", lambda e: e.tensor_reduce(
                    out=foldt[:], in_=gpmb[:, 0:2 * ICN], axis=AX, op=OP.max),
                    waits=[fw3a], inc="vS")
                fw3b = fold_to_scalar(foldt[:], ("vS", fp), 1)
                b3w = collective(2, fw3b)
                def d3(e):
                    e.tensor_scalar(out=srow[:, 7:8], in0=ccs[0:1, 0:1], scalar1=srow[:, 1:2],
                                    scalar2=1.0 / 32767.0, op0=OP.mult, op1=OP.mult)
                    e.tensor_scalar(out=srow[:, 8:9], in0=ccs[0:1, 1:2], scalar1=srow[:, 3:4],
                                    scalar2=(2.0 ** -16) / 32767.0, op0=OP.mult, op1=OP.mult)
                    e.drain()
                    e.tensor_scalar(out=srow[:, 7:8], in0=srow[:, 7:8], scalar1=1e-12,
                                    scalar2=None, op0=OP.max)
                    e.tensor_scalar(out=srow[:, 8:9], in0=srow[:, 8:9], scalar1=1e-12,
                                    scalar2=None, op0=OP.max)
                    e.drain()
                    e.reciprocal(out=srow[:, 4:5], in_=srow[:, 7:8])
                    e.reciprocal(out=srow[:, 5:6], in_=srow[:, 8:9])
                    e.drain()
                    e.tensor_scalar(out=srow[:, 4:5], in0=srow[:, 4:5], scalar1=srow[:, 1:2],
                                    scalar2=None, op0=OP.mult)
                    e.tensor_scalar(out=srow[:, 5:6], in0=srow[:, 5:6], scalar1=srow[:, 3:4],
                                    scalar2=2.0 ** -16, op0=OP.mult, op1=OP.mult)
                    return e.drain()
                dv3 = sch.emit("vector", d3, waits=[b3w], inc="vS")
                bc3 = bcast_scal(("vS", dv3))
                sch.barrier()

                cut(55)
                wub_free = [("tS", sch.val("tS")), ("tS", sch.val("tS"))]
                gb_free = [("vS", sch.val("vS")), ("vS", sch.val("vS"))]
                op_free = [None, None]
                op_sp_done = [None] * ICN
                chain_free2 = [None, None]
                for ic in range(ICN):
                    pid = ic % 2
                    wsem = "wA" if pid == 0 else "wB"
                    lw = sch.emit("sync", lambda e, ic=ic, pid=pid: e.dma_start(
                        out=wt_bf[:, pid, :, :].rearrange("p c m -> p (c m)"), in_=d_wub[ic]),
                        waits=[wub_free[pid]], inc=wsem, amt=16)
                    sop = None
                    for hf in range(2):
                        hpid = (2 * ic + hf) % 2
                        gsem = "xA" if hpid == 0 else "xB"
                        lg = sch.emit("sync", lambda e, ic=ic, hf=hf, hpid=hpid: e.dma_start(
                            out=gbuf[:, hpid, :], in_=d_gp[ic, :, hf * 2048:(hf + 1) * 2048]),
                            waits=[gb_free[hpid], gp_sp_done[ic]], inc=gsem, amt=16)
                        sg2 = sch.emit("scalar", lambda e, hpid=hpid: e.activation(
                            out=sgb2[:, hpid, :], in_=gbuf[:, hpid, :], func=ACT.Identity,
                            bias=magicT[:], scale=SCAL[:, 5:6]),
                            waits=[(gsem, lg), bc3, chain_free2[hpid]], inc="sS")
                        vg2 = sch.emit("vector", lambda e, hpid=hpid: e.tensor_scalar(
                            out=sgb2[:, hpid, :], in0=sgb2[:, hpid, :], scalar1=MAGIC,
                            scalar2=None, op0=OP.subtract),
                            waits=[("sS", sg2)], inc="vS")
                        gb_free[hpid] = ("vS", vg2)
                        vlast = None
                        for tci in range(4):
                            tc = hf * 4 + tci
                            bpid = tc % 4
                            tm = sch.emit("tensor", lambda e, tc=tc, pid=pid, bpid=bpid: [
                                e.matmul(psUG[bpid][:], wt_bf[:, pid, c, :],
                                         xr_fm[:, c, tc * 512:(tc + 1) * 512],
                                         start=(c == 0), stop=(c == DC - 1)) for c in range(DC)][-1],
                                waits=[(wsem, lw)] + (ug_free[bpid] if isinstance(ug_free[bpid], list)
                                                      else [ug_free[bpid]]), inc="tS")
                            s1 = sch.emit("scalar", lambda e, tci=tci, hpid=hpid, bpid=bpid:
                                          e.activation(
                                out=grb2[:, hpid, tci * 512:(tci + 1) * 512], in_=psUG[bpid][:],
                                func=ACT.Identity, bias=magicT[:], scale=SCAL[:, 4:5]),
                                waits=[("tS", tm), chain_free2[hpid]], inc="sS")
                            vlast = sch.emit("vector", lambda e, tci=tci, hpid=hpid: e.tensor_scalar(
                                out=grb2[:, hpid, tci * 512:(tci + 1) * 512],
                                in0=grb2[:, hpid, tci * 512:(tci + 1) * 512],
                                scalar1=MAGIC, scalar2=None, op0=OP.subtract),
                                waits=[("sS", s1)], inc="vS")
                            ug_free[bpid] = [("sS", s1)]
                        gop = sch.emit("gpsimd", lambda e, hpid=hpid: e.tensor_tensor(
                            out=gpb[:, hpid, :], in0=sgb2[:, hpid, :], in1=grb2[:, hpid, :],
                            op=OP.mult),
                            waits=[("vS", vlast), op_free[hpid]], inc="gS")
                        chain_free2[hpid] = ("gS", gop)
                        vom = sch.emit("vector", lambda e, ic=ic, hf=hf, hpid=hpid: e.tensor_reduce(
                            out=opmb[:, 2 * ic + hf: 2 * ic + hf + 1], in_=gpb[:, hpid, :],
                            axis=AX, op=OP.max, apply_absolute_value=True),
                            waits=[("gS", gop)], inc="vS")
                        sop = sch.emit("gpsimd", lambda e, ic=ic, hf=hf, hpid=hpid: e.dma_start(
                            out=d_op[hf * 8:(hf + 1) * 8, :, ic, :].rearrange(
                                "b p t -> p b t"),
                            in_=gpb[:, hpid, :].rearrange("p (b t) -> p b t", t=256)),
                            waits=[("vS", vom)], inc="gd", amt=16)
                        op_free[hpid] = ("gd", sop)
                    wub_free[pid] = ("tS", sch.val("tS"))
                    op_sp_done[ic] = ("gd", sop)

                fo = sch.emit("vector", lambda e: e.tensor_reduce(
                    out=foldt[:], in_=opmb[:, 0:2 * ICN], axis=AX, op=OP.max), inc="vS")
                fw4 = fold_to_scalar(foldt[:], ("vS", fo), 0)
                b4w = collective(3, fw4)
                def d4(e):
                    e.tensor_scalar(out=srow[:, 9:10], in0=srow[:, 7:8], scalar1=srow[:, 8:9],
                                    scalar2=None, op0=OP.mult)
                    e.drain()
                    e.tensor_scalar(out=srow[:, 10:11], in0=ccs[0:1, 0:1], scalar1=srow[:, 9:10],
                                    scalar2=1.0 / 32767.0, op0=OP.mult, op1=OP.mult)
                    e.drain()
                    e.tensor_scalar(out=srow[:, 10:11], in0=srow[:, 10:11], scalar1=1e-12,
                                    scalar2=None, op0=OP.max)
                    e.drain()
                    e.reciprocal(out=srow[:, 6:7], in_=srow[:, 10:11])
                    e.drain()
                    e.tensor_scalar(out=srow[:, 6:7], in0=srow[:, 6:7], scalar1=srow[:, 9:10],
                                    scalar2=None, op0=OP.mult)
                    return e.drain()
                dv4 = sch.emit("vector", d4, waits=[b4w], inc="vS")
                bc4 = bcast_scal(("vS", dv4))
                sch.barrier()
            op_sd = ("gd", sch.val("gd"))

            cut(60)
            with contextlib.ExitStack() as es5:
                E5 = es5.enter_context
                wdq_raw = E5(nc.sbuf_tensor("wdq_raw", [128, II], F32))
                wd_bmax = E5(nc.sbuf_tensor("wd_bmax", [128, 96], F32))
                wd_binv = E5(nc.sbuf_tensor("wd_binv", [128, 96], F32))
                wd_q1 = E5(nc.sbuf_tensor("wd_q1", [128, II], F32))
                wd_qb = E5(nc.sbuf_tensor("wd_qb", [128, II], BF16))
                wdt_bf = E5(nc.sbuf_tensor("wdt_bf", [128, ICN, D], BF16))
                opst = E5(nc.sbuf_tensor("opst", [128, 2, ICN, 256], F32))
                orb = E5(nc.sbuf_tensor("orb", [128, 2, ICN, 256], BF16))
                oa = E5(nc.sbuf_tensor("oa", [128, 2, 256], F32))
                lres = E5(nc.sbuf_tensor("lres", [128, 2, D], F32))
                otile = E5(nc.sbuf_tensor("otile", [128, 2, D], F32))
                bnb2 = E5(nc.sbuf_tensor("bnb2", [128, 2, 2, 6], F32))
                mvb2 = E5(nc.sbuf_tensor("mvb2", [128, 2, 2, 2], F32))
                lnsc2 = E5(nc.sbuf_tensor("lnsc2", [128, 2, 6], F32))
                meanb2 = E5(nc.sbuf_tensor("meanb2", [128, 2], F32))
                varb2 = E5(nc.sbuf_tensor("varb2", [128, 2], F32))
                rstdb2 = E5(nc.sbuf_tensor("rstdb2", [128, 2], F32))
                psWD = [E5(nc.psum_tensor(un(f"psWD{i}"), [128, 8, 128], BF16)) for i in range(2)]
                psDN = [E5(nc.psum_tensor(un(f"psDN{i}"), [128, 2, 512], F32)) for i in range(2)]

                wdr_free = [None]
                wdq1_free = [None]
                wdps_free = [None, None]
                wdqb_free = [None, None]
                for ht in range(DC):
                    pid = ht % 2
                    lv = sch.emit("sync", lambda e, ht=ht: e.dma_start(
                        out=wdq_raw[:], in_=dw["wdown"][ht]),
                        waits=[wdr_free[0]], inc="wA", amt=16)
                    def vq(e):
                        e.tensor_reduce(out=wd_bmax[:],
                                        in_=wdq_raw[:].rearrange("p (b k) -> p b k", k=32),
                                        axis=AX, op=OP.max, apply_absolute_value=True)
                        e.drain()
                        e.tensor_scalar(out=wd_bmax[:], in0=wd_bmax[:],
                                        scalar1=1.0 / 7.0, scalar2=1e-12, op0=OP.mult, op1=OP.max)
                        e.drain()
                        e.reciprocal(out=wd_binv[:], in_=wd_bmax[:])
                        e.drain()
                        return e.tensor_tensor(
                            out=wd_q1[:].rearrange("p (b k) -> p b k", k=32),
                            in0=wdq_raw[:].rearrange("p (b k) -> p b k", k=32),
                            in1=wd_binv[:, :, None].broadcast_to((128, 96, 32)), op=OP.mult)
                    v1 = sch.emit("vector", vq, waits=[("wA", lv), wdq1_free[0]], inc="vS")
                    s1 = sch.emit("scalar", lambda e: e.activation(
                        out=wd_q1[:], in_=wd_q1[:], func=ACT.Identity, bias=magicT[:], scale=1.0),
                        waits=[("vS", v1)], inc="sS")
                    s2 = sch.emit("scalar", lambda e: e.activation(
                        out=wd_q1[:], in_=wd_q1[:], func=ACT.Identity, bias=nmagicT[:], scale=1.0),
                        inc="sS")
                    v2 = sch.emit("vector", lambda e: e.tensor_tensor(
                        out=wd_qb[:].rearrange("p (b k) -> p b k", k=32),
                        in0=wd_q1[:].rearrange("p (b k) -> p b k", k=32),
                        in1=wd_bmax[:, :, None].broadcast_to((128, 96, 32)), op=OP.mult),
                        waits=[("sS", s2), wdqb_free[0]], inc="vS")
                    wdq1_free[0] = ("vS", v2)
                    wdr_free[0] = ("vS", v1)
                    last_tv = None
                    for icg in range(3):
                        tpid = icg % 2
                        tv = sch.emit("tensor", lambda e, icg=icg, tpid=tpid: [
                            e.transpose(psWD[tpid][:, c, :],
                                        wd_qb[:, (icg * 8 + c) * 128:(icg * 8 + c + 1) * 128],
                                        ident_bf[:]) for c in range(8)][-1],
                            waits=[("vS", v2), wdps_free[tpid]], inc="tS")
                        vv = sch.emit("vector", lambda e, ht=ht, icg=icg, tpid=tpid: e.tensor_copy(
                            out=wdt_bf[:, icg * 8:(icg + 1) * 8, ht * 128:(ht + 1) * 128],
                            in_=psWD[tpid][:]),
                            waits=[("tS", tv)], inc="vS")
                        wdps_free[tpid] = ("vS", vv)
                        last_tv = tv
                    wdqb_free[0] = ("tS", last_tv)
                wd_done = ("vS", sch.val("vS"))

                ops_free = [None, None]
                lres_free = [None, None]
                dn_free = [None, None]
                ot_free = [None, None]
                oa_free = [None, None]
                orb_free = [None, None]
                ord_mark = {}
                NB = 16

                def rounds(tb):
                    pid = tb % 2
                    osem = "xA" if pid == 0 else "xB"
                    lo = sch.emit("sync", lambda e, tb=tb, pid=pid: e.dma_start(
                        out=opst[:, pid, :, :], in_=d_op[tb]),
                        waits=[ops_free[pid], op_sd], inc=osem, amt=16)
                    for ic in range(ICN):
                        apid = ic % 2
                        s1 = sch.emit("scalar", lambda e, ic=ic, pid=pid, apid=apid: e.activation(
                            out=oa[:, apid, :], in_=opst[:, pid, ic, :], func=ACT.Identity,
                            bias=magicT[:], scale=SCAL[:, 6:7]),
                            waits=[(osem, lo), bc4, oa_free[apid]], inc="sS")
                        v1 = sch.emit("vector", lambda e, ic=ic, pid=pid, apid=apid: e.tensor_scalar(
                            out=orb[:, pid, ic, :], in0=oa[:, apid, :], scalar1=MAGIC, scalar2=None,
                            op0=OP.subtract),
                            waits=[("sS", s1), orb_free[pid] if ic == 0 else None], inc="vS")
                        oa_free[apid] = ("vS", v1)
                    ops_free[pid] = ("sS", sch.val("sS"))
                    ord_mark[tb] = ("vS", sch.val("vS"))

                rounds(0)
                for tb in range(NB):
                    pid = tb % 2
                    if tb + 1 < NB:
                        rounds(tb + 1)
                    if dbg and tb == 0:
                        sch.emit("gpsimd", lambda e: e.dma_start(out=d_dbg_or, in_=orb[:, 0]),
                                 waits=[ord_mark[0]], inc="gd", amt=16)
                    for ti in range(2):
                        tt32 = tb * 2 + ti
                        dpid = ti % 2
                        rsem = "pA" if dpid == 0 else "pB"
                        lr = sch.emit("sync", lambda e, tt32=tt32, dpid=dpid: e.dma_start(
                            out=lres[:, dpid, :], in_=d_ln1[tt32]),
                            waits=[lres_free[dpid], ln1_sd], inc=rsem, amt=16)
                        tdn = sch.emit("tensor", lambda e, ti=ti, pid=pid, dpid=dpid: [
                            e.matmul(psDN[dpid][:, nh, :], orb[:, pid, ic, ti * 128:(ti + 1) * 128],
                                     wdt_bf[:, ic, nh * 512:(nh + 1) * 512],
                                     start=(ic == 0), stop=(ic == ICN - 1))
                            for nh in range(2) for ic in range(ICN)][-1],
                            waits=[ord_mark[tb], wd_done, dn_free[dpid]], inc="tS")
                        vres = sch.emit("vector", lambda e, ti=ti, dpid=dpid: [
                            e.scalar_tensor_tensor(
                                out=otile[:, ti, nh * 512:(nh + 1) * 512],
                                in0=psDN[dpid][:, nh, :], scalar=SCAL[:, 10:11],
                                in1=lres[:, dpid, nh * 512:(nh + 1) * 512],
                                op0=OP.mult, op1=OP.add) for nh in range(2)][-1],
                            waits=[(rsem, lr), ("tS", tdn), ot_free[ti]], inc="vS")
                        dn_free[dpid] = ("vS", vres)
                        lres_free[dpid] = ("vS", vres)
                        if dbg and tb == 0:
                            sch.emit("gpsimd", lambda e, ti=ti: e.dma_start(
                                out=d_dbg_ffn[ti], in_=otile[:, ti, :]),
                                waits=[("vS", sch.val("vS"))], inc="gd", amt=16)
                        sch.emit("vector", lambda e, ti=ti: [
                            e.bn_stats(out=bnb2[:, ti, 0, :], in_=otile[:, ti, 0:512]),
                            e.bn_stats(out=bnb2[:, ti, 1, :], in_=otile[:, ti, 512:1024])][-1],
                            inc="vS")
                    orb_free[pid] = ("tS", sch.val("tS"))
                    def combine2(e):
                        for ti in range(2):
                            e.bn_aggr(out=mvb2[:, ti, 0, :], in_=bnb2[:, ti, 0, :])
                            e.bn_aggr(out=mvb2[:, ti, 1, :], in_=bnb2[:, ti, 1, :])
                        e.drain()
                        e.tensor_tensor(out=lnsc2[:, :, 0], in0=mvb2[:, :, 0, 0],
                                        in1=mvb2[:, :, 1, 0], op=OP.subtract)
                        e.tensor_tensor(out=lnsc2[:, :, 2], in0=mvb2[:, :, 0, 0],
                                        in1=mvb2[:, :, 1, 0], op=OP.add)
                        e.tensor_tensor(out=lnsc2[:, :, 3], in0=mvb2[:, :, 0, 1],
                                        in1=mvb2[:, :, 1, 1], op=OP.add)
                        e.drain()
                        e.tensor_scalar(out=lnsc2[:, :, 1], in0=lnsc2[:, :, 0], scalar1=0.5,
                                        scalar2=None, op0=OP.mult)
                        e.tensor_scalar(out=meanb2[:], in0=lnsc2[:, :, 2], scalar1=0.5,
                                        scalar2=None, op0=OP.mult)
                        e.drain()
                        e.tensor_tensor(out=lnsc2[:, :, 4], in0=lnsc2[:, :, 1], in1=lnsc2[:, :, 1],
                                        op=OP.mult)
                        e.drain()
                        return e.scalar_tensor_tensor(
                            out=varb2[:], in0=lnsc2[:, :, 3], scalar=0.5,
                            in1=lnsc2[:, :, 4], op0=OP.mult, op1=OP.add)
                    cv2 = sch.emit("vector", combine2, inc="vS")
                    sq2 = sch.emit("scalar", lambda e: e.activation(
                        out=varb2[:], in_=varb2[:], func=ACT.Sqrt, bias=epsT[:], scale=1.0),
                        waits=[("vS", cv2)], inc="sS")
                    sch.emit("vector", lambda e: e.reciprocal(out=rstdb2[:], in_=varb2[:]),
                             waits=[("sS", sq2)], inc="vS")
                    dr2b = sch.emit("vector", lambda e: e.drain(), inc="vS")
                    for ti in range(2):
                        tt32 = tb * 2 + ti
                        b_i, trow = divmod(tt32, 16)
                        nv = sch.emit("vector", lambda e, ti=ti: e.tensor_scalar(
                            out=otile[:, ti, :], in0=otile[:, ti, :],
                            scalar1=meanb2[:, ti:ti + 1], scalar2=rstdb2[:, ti:ti + 1],
                            op0=OP.subtract, op1=OP.mult), waits=[("vS", dr2b)], inc="vS")
                        sch.emit("gpsimd", lambda e, ti=ti, b_i=b_i, trow=trow: e.dma_start(
                            out=d_out[b_i, trow * 128:(trow + 1) * 128, :], in_=otile[:, ti, :]),
                            waits=[("vS", nv)], inc="so", amt=16)
                        ot_free[ti] = ("so", sch.val("so"))

                if dbg:
                    sch.emit("gpsimd", lambda e: e.dma_start(out=d_dbg_scal, in_=srow[:]),
                             waits=[("vS", sch.val("vS"))], inc="gd", amt=16)
                fin = [("so", sch.val("so")), ("gd", sch.val("gd")), ("sd", sch.val("sd"))]
                sch.emit("sync", lambda e: e.nop(), waits=fin)


        except _Cut:
            pass
        if sch.cutoff is not None:
            co = sch.cutoff
            tot = {s: 0 for s in ALLSEMS}
            for e in ENGS:
                kept = [op for op in sch.ops[e] if op[0] < co]
                sch.ops[e] = kept
                for _, _, inc, amt, _f in kept:
                    if inc is not None:
                        tot[inc] += amt
            sch.cnt = tot
        sch.emit("sync", lambda e: e.nop(),
                 waits=[(s, sch.val(s)) for s in ("ld", "sd", "gd", "so", "cc",
                                                  "tS", "vS", "sS", "gS",
                                                  "xA", "xB", "wA", "wB", "pA", "pB")])

        with nc.Block() as block:
            def mk(engname):
                def body(e):
                    for _seq, w, inc, amt, fn in sch.ops[engname]:
                        for (s, v) in w:
                            e.wait_ge(sems[s], v)
                        ins = fn(e)
                        if inc is not None:
                            if not hasattr(ins, "then_inc"):
                                ins = e.nop()
                            ins.then_inc(sems[inc], amt)
                return body
            block.tensor(mk("tensor"))
            block.vector(mk("vector"))
            block.scalar(mk("scalar"))
            block.gpsimd(mk("gpsimd"))
            block.sync(mk("sync"))
    return nc


def _host_prep(inputs):
    g = lambda k: np.asarray(inputs[k])
    x = np.ascontiguousarray(np.asarray(g("L0_attn_x"), np.float32))
    wup = np.ascontiguousarray(np.asarray(g("L0_ffn_w_up"), np.float32).reshape(ICN, 128, D))
    wgate = np.ascontiguousarray(np.asarray(g("L0_ffn_w_gate"), np.float32).reshape(ICN, 128, D))
    wdown = np.ascontiguousarray(np.asarray(g("L0_ffn_w_down"), np.float32).reshape(DC, 128, II))
    ident = np.eye(128, dtype=np.float32)
    r16 = np.zeros((128, S), dtype=np.float32)
    r16[np.arange(S) // 16, np.arange(S)] = 1.0
    b2 = lambda v: np.ascontiguousarray(np.asarray(v, np.float32).reshape(EC, 128).T)
    common = {
        "wi": g("L0_attn_Wi"), "wq": g("L0_attn_Wq"), "wk": g("L0_attn_Wk"),
        "wv": g("L0_attn_Wv"), "wp": g("L0_attn_Wp"), "w1": g("L0_attn_W1"),
        "w2": g("L0_attn_W2"), "wo": g("L0_attn_Wo"),
        "wh2": np.ascontiguousarray(np.asarray(g("L0_attn_Wh"), np.float32)[:, 2]
                                    .reshape(EC, 128).T),
        "cwT": np.asarray(g("L0_attn_conv_w"), np.float32).T,
        "bi": b2(g("L0_attn_bi")), "bq": b2(g("L0_attn_bq")), "bk": b2(g("L0_attn_bk")),
        "bv1": np.asarray(g("L0_attn_bv"), np.float32).reshape(1, E),
        "bp": b2(g("L0_attn_bp")), "b1": b2(g("L0_attn_b1")), "b2": b2(g("L0_attn_b2")),
        "bo1": np.asarray(g("L0_attn_bo"), np.float32).reshape(1, D),
        "cb": b2(g("L0_attn_conv_b")),
        "bh2": np.asarray(g("L0_attn_bh"), np.float32)[2].reshape(1, 1),
        "wup": wup, "wgate": wgate, "wdown": wdown,
        "ident": ident, "r16": r16,
    }
    common = {k: np.ascontiguousarray(np.asarray(v, np.float32)) for k, v in common.items()}
    maps = []
    for c in range(NCORES):
        m = dict(common)
        m["x"] = np.ascontiguousarray(x[B * c:B * (c + 1)])
        maps.append(m)
    return maps


_NC_CACHE = {}


def kernel(**inputs):
    maps = _host_prep(inputs)
    if "nc" not in _NC_CACHE:
        _NC_CACHE["nc"] = build_nc(dbg=False)
    nc = _NC_CACHE["nc"]
    res = run_bass_kernel_spmd(nc, maps, core_ids=list(range(NCORES)))
    out = np.concatenate([np.asarray(r["out"]) for r in res.results], axis=0)
    return np.ascontiguousarray(out.astype(np.float32))


# version 19
# speedup vs baseline: 1.0058x; 1.0058x over previous
import numpy as np
import contextlib
import concourse.bass as bass
import concourse.mybir as mybir
from concourse.bass_utils import run_bass_kernel_spmd

F32 = mybir.dt.float32
BF16 = mybir.dt.bfloat16
AX = mybir.AxisListType.X
OP = mybir.AluOpType
ACT = mybir.ActivationFunctionType
MAGIC = float(1.5 * 2.0**23)
NEG = -1e30

NCORES = 8
B = 2
S = 2048
NT = 16
D = 1024
DC = 8
E = 256
EC = 2
LOOK = 64
II = 3072
ICN = 24
TPC = 4096
NT32 = 32

ENGS = ("tensor", "vector", "scalar", "gpsimd", "sync")
ESEM = {"tensor": "tS", "vector": "vS", "scalar": "sS", "gpsimd": "gS", "sync": "yS"}
ALLSEMS = ("ld", "sd", "gd", "so", "cc", "tS", "vS", "sS", "gS", "yS",
           "xA", "xB", "wA", "wB", "pA", "pB")


class Sched:
    def __init__(self):
        self.ops = {e: [] for e in ENGS}
        self.cnt = {s: 0 for s in ALLSEMS}
        self.pending = {e: [] for e in ENGS}
        self.seq = 0
        self.cutoff = None

    def val(self, sem):
        return self.cnt[sem]

    def emit(self, eng, fn, waits=(), inc=None, amt=1):
        w = list(self.pending[eng])
        self.pending[eng] = []
        for x in waits:
            if x is None:
                continue
            s, v = x
            if s == ESEM[eng] or v <= 0:
                continue
            w.append((s, v))
        dd = {}
        for s, v in w:
            dd[s] = max(dd.get(s, 0), v)
        w = sorted(dd.items())
        newv = None
        if inc is not None:
            self.cnt[inc] += amt
            newv = self.cnt[inc]
        self.ops[eng].append((self.seq, w, inc, amt, fn))
        self.seq += 1
        return newv

    def mark(self, eng):
        return (ESEM[eng], self.val(ESEM[eng]))

    def barrier(self):
        marks = [(s, self.cnt[s]) for s in ("ld", "sd", "gd", "so", "tS", "vS", "sS", "gS",
                                            "xA", "xB", "wA", "wB", "pA", "pB")]
        for e in ENGS:
            self.pending[e] = [(s, v) for (s, v) in marks if s != ESEM[e] and v > 0]


class _Cut(Exception):
    pass


def build_nc(dbg=False, phase=99):
    nc = bass.Bass("TRN2", target_bir_lowering=False, debug=False)
    sch = Sched()
    def cut(k):
        if phase < k and sch.cutoff is None:
            sch.cutoff = sch.seq
    _uc = [0]
    def un(base):
        _uc[0] += 1
        return f"{base}_{_uc[0]}"

    d_x = nc.dram_tensor("x", [B, S, D], F32, kind="ExternalInput").ap()
    d_out = nc.dram_tensor("out", [B, S, D], F32, kind="ExternalOutput").ap()
    dw = {}
    for nm, shp in [
        ("wi", [D, E]), ("wq", [E, E]), ("wk", [E, E]), ("wv", [E, E]),
        ("wp", [E, E]), ("w1", [D, E]), ("w2", [E, E]), ("wo", [E, D]),
        ("wh2", [128, EC]), ("cwT", [E, 5]),
        ("bi", [128, EC]), ("bq", [128, EC]), ("bk", [128, EC]), ("bv1", [1, E]),
        ("bp", [128, EC]), ("b1", [128, EC]), ("b2", [128, EC]),
        ("bo1", [1, D]), ("cb", [128, EC]), ("bh2", [1, 1]),
        ("wup", [ICN, 128, D]), ("wgate", [ICN, 128, D]), ("wdown", [DC, 128, II]),
        ("ident", [128, 128]), ("r16", [128, S]),
    ]:
        dw[nm] = nc.dram_tensor(nm, shp, F32, kind="ExternalInput").ap()

    d_ln1 = nc.dram_tensor("ln1sp", [NT32, 128, D], F32, kind="Internal").ap()
    d_gr = nc.dram_tensor("grsp", [ICN, 128, TPC], F32, kind="Internal").ap()
    d_gp = nc.dram_tensor("gpsp", [ICN, 128, TPC], F32, kind="Internal").ap()
    d_op = nc.dram_tensor("opsp", [16, 128, ICN, 256], F32, kind="Internal").ap()
    d_wub = nc.dram_tensor("wubsp", [ICN, 128, DC * 128], BF16, kind="Internal").ap()
    cc_in = [nc.dram_tensor(f"cci{k}", [1, 2], F32, kind="Internal") for k in range(4)]
    cc_out = [nc.dram_tensor(f"cco{k}", [1, 2], F32, kind="Internal") for k in range(4)]
    if dbg:
        d_dbg_gum = nc.dram_tensor("dbg_gum", [2, 128, 192], F32, kind="ExternalOutput").ap()
        d_dbg_xr = nc.dram_tensor("dbg_xr", [3, 128, DC, 512], BF16, kind="ExternalOutput").ap()
        d_dbg_s2 = nc.dram_tensor("dbg_s2", [1, 16], F32, kind="ExternalOutput").ap()
        d_dbg_ln1 = nc.dram_tensor("dbg_ln1", [NT32, 128, D], F32, kind="ExternalOutput").ap()
        d_dbg_h = nc.dram_tensor("dbg_h", [B, 128, EC, S], F32, kind="ExternalOutput").ap()
        d_dbg_scal = nc.dram_tensor("dbg_scal", [1, 16], F32, kind="ExternalOutput").ap()
        d_dbg_or = nc.dram_tensor("dbg_or", [128, ICN, 256], BF16, kind="ExternalOutput").ap()
        d_dbg_ffn = nc.dram_tensor("dbg_ffn", [2, 128, D], F32, kind="ExternalOutput").ap()

    with contextlib.ExitStack() as ES:
        EE = ES.enter_context

        wi_bf = EE(nc.sbuf_tensor("wi_bf", [128, DC, E], BF16))
        wq_bf = EE(nc.sbuf_tensor("wq_bf", [128, EC, E], BF16))
        wk_bf = EE(nc.sbuf_tensor("wk_bf", [128, EC, E], BF16))
        wv_bf = EE(nc.sbuf_tensor("wv_bf", [128, EC, E], BF16))
        wp_bf = EE(nc.sbuf_tensor("wp_bf", [128, EC, E], BF16))
        w1_bf = EE(nc.sbuf_tensor("w1_bf", [128, DC, E], BF16))
        w1k_bf = EE(nc.sbuf_tensor("w1k_bf", [128, 5, EC, E], BF16))
        w2_bf = EE(nc.sbuf_tensor("w2_bf", [128, EC, E], BF16))
        wo_bf = EE(nc.sbuf_tensor("wo_bf", [128, EC, D], BF16))
        wh2_bf = EE(nc.sbuf_tensor("wh2_bf", [128, EC, 1], BF16))
        cwT_s = EE(nc.sbuf_tensor("cwT_s", [128, EC, 5], F32))
        ident_bf = EE(nc.sbuf_tensor("ident_bf", [128, 128], BF16))
        ident32 = EE(nc.sbuf_tensor("ident32", [128, 128], F32))
        r16_bf = EE(nc.sbuf_tensor("r16_bf", [128, S], BF16))
        ones_f = EE(nc.sbuf_tensor("ones_f", [1, 128], F32))
        bias = {nm: EE(nc.sbuf_tensor("bs_" + nm, [128, EC, 1], F32))
                for nm in ("bi", "bq4", "bk4", "b1e", "b2")}
        bh2_s = EE(nc.sbuf_tensor("bh2_s", [1, 1], F32))
        BV = EE(nc.sbuf_tensor("BV", [128, E], F32))
        BO = EE(nc.sbuf_tensor("BO", [128, D], F32))
        srow = EE(nc.sbuf_tensor("srow", [1, 16], F32))
        SCAL = EE(nc.sbuf_tensor("SCAL", [128, 16], F32))
        epsT = EE(nc.sbuf_tensor("epsT", [128, 1], F32))
        magicT = EE(nc.sbuf_tensor("magicT", [128, 1], F32))
        nmagicT = EE(nc.sbuf_tensor("nmagicT", [128, 1], F32))
        MASK = EE(nc.sbuf_tensor("MASK", [128, 192], F32))
        MASK0 = EE(nc.sbuf_tensor("MASK0", [128, 192], F32))
        ccs = EE(nc.sbuf_tensor("ccs", [1, 2], F32))
        foldt = EE(nc.sbuf_tensor("foldt", [128, 1], F32))

        sems = {s: EE(nc.semaphore(s)) for s in ALLSEMS if s != "yS"}
        sems["yS"] = EE(nc.semaphore("yS"))

        psBC = EE(nc.psum_tensor(un("psBC"), [128, 16], F32))
        psXP = EE(nc.psum_tensor(un("psXP"), [1, 128], F32))

        try:
            with contextlib.ExitStack() as esI:
                EI = esI.enter_context
                wstg = EI(nc.sbuf_tensor("wstg", [128, DC, E], F32))
                braw = {nm: EI(nc.sbuf_tensor("braw_" + nm, [128, EC, 1], F32))
                        for nm in ("bi", "bq", "bk", "bp", "b1", "b2", "cb")}
                bv1 = EI(nc.sbuf_tensor("bv1_s", [1, E], F32))
                bo1 = EI(nc.sbuf_tensor("bo1_s", [1, D], F32))
                w1f = EI(nc.sbuf_tensor("w1f", [128, 4, E], F32))
                psI = EI(nc.psum_tensor(un("psI"), [128, D], F32))
                psBE = EI(nc.psum_tensor(un("psBE"), [128, EC, 1], F32))

                cast_free = [None]

                def ld_cast(dsrc, cview, eview, bf_dst):
                    stg = wstg[:].rearrange("p c e -> p (c e)")[:, 0:cview * eview]
                    stg = stg.rearrange("p (c e) -> p c e", c=cview)
                    lv = sch.emit("sync", lambda e, d=dsrc, t=stg: e.dma_start(out=t, in_=d),
                                  waits=[cast_free[0]], inc="ld", amt=16)
                    c = sch.emit("vector", lambda e, t=stg, o=bf_dst: e.tensor_copy(out=o, in_=t),
                                 waits=[("ld", lv)], inc="vS")
                    cast_free[0] = ("vS", c)

                ld_cast(dw["wi"].rearrange("(c p) e -> p c e", p=128), DC, E, wi_bf[:])
                for nm, dst in (("wq", wq_bf), ("wk", wk_bf), ("wv", wv_bf),
                                ("wp", wp_bf), ("w2", w2_bf)):
                    ld_cast(dw[nm].rearrange("(c p) e -> p c e", p=128), EC, E, dst[:])
                ld_cast(dw["w1"].rearrange("(c p) e -> p c e", p=128), DC, E, w1_bf[:])
                ld_cast(dw["wo"].rearrange("(c p) d -> p c d", p=128), EC, D, wo_bf[:])
                ld_cast(dw["wh2"].rearrange("p (c o) -> p c o", c=EC), EC, 1, wh2_bf[:])
                ld_cast(dw["ident"].rearrange("p (c m) -> p c m", c=1), 1, 128,
                        ident_bf[:].rearrange("p (c m) -> p c m", c=1))
                ld_cast(dw["r16"].rearrange("p (c m) -> p c m", c=1), 1, S,
                        r16_bf[:].rearrange("p (c m) -> p c m", c=1))
                for dsrc, dst in [(dw["cwT"].rearrange("(c p) k -> p c k", p=128), cwT_s[:]),
                                  (dw["ident"], ident32[:]), (dw["bh2"], bh2_s[:]),
                                  (dw["bv1"], bv1[:]), (dw["bo1"], bo1[:])]:
                    sch.emit("sync", lambda e, d=dsrc, t=dst: e.dma_start(out=t, in_=d),
                             inc="ld", amt=16)
                for nm in braw:
                    sch.emit("sync", lambda e, d=dw[nm], t=braw[nm]: e.dma_start(
                        out=t[:], in_=d.rearrange("p (c o) -> p c o", o=1)), inc="ld", amt=16)
                lw1f = sch.emit("sync", lambda e: e.dma_start(
                    out=w1f[:], in_=dw["w1"].rearrange("(c p) e -> p c e", p=128)[:, 2:6, :]),
                    inc="ld", amt=16)
                all_ld = ("ld", sch.val("ld"))

                def vconsts(e):
                    e.memset(ones_f[:], 1.0)
                    e.memset(epsT[:], 1e-5)
                    e.memset(magicT[:], MAGIC)
                    e.memset(nmagicT[:], -MAGIC)
                    e.memset(SCAL[:], 0.0)
                    e.memset(srow[:], 0.0)
                    e.tensor_copy(out=bias["bi"][:], in_=braw["bi"][:])
                    e.tensor_copy(out=bias["b2"][:], in_=braw["b2"][:])
                    e.tensor_scalar(out=bias["bq4"][:], in0=braw["bq"][:], scalar1=0.25,
                                    scalar2=None, op0=OP.mult)
                    e.tensor_scalar(out=bias["bk4"][:], in0=braw["bk"][:], scalar1=0.25,
                                    scalar2=None, op0=OP.mult)
                    _l = None
                    for k in range(5):
                        for c in range(EC):
                            _l = e.tensor_scalar(out=w1k_bf[:, k, c, :], in0=w1_bf[:, 2 + c, :],
                                                 scalar1=cwT_s[:, c, k:k + 1], scalar2=None,
                                                 op0=OP.mult)
                    return _l
                vc = sch.emit("vector", vconsts, waits=[all_ld], inc="vS")

                def gmasks(e):
                    e.memset(MASK[:], 0.0)
                    e.affine_select(out=MASK[:], in_=MASK[:], compare_op=OP.is_ge,
                                    fill=NEG, base=0, pattern=[[1, 192]], channel_multiplier=-1)
                    e.affine_select(out=MASK[:], in_=MASK[:], compare_op=OP.is_ge,
                                    fill=NEG, base=LOOK - 1, pattern=[[-1, 192]], channel_multiplier=1)
                    e.tensor_copy(out=MASK0[:], in_=MASK[:])
                    return e.affine_select(out=MASK0[:], in_=MASK0[:], compare_op=OP.is_ge,
                                           fill=NEG, base=-LOOK, pattern=[[1, 192]],
                                           channel_multiplier=0)
                sch.emit("gpsimd", gmasks, inc="gS")

                t1 = sch.emit("tensor", lambda e: e.matmul(psI[:, 0:E], ones_f[:], bv1[:],
                                                           start=True, stop=True),
                              waits=[all_ld, ("vS", vc)], inc="tS")
                v1 = sch.emit("vector", lambda e: e.tensor_copy(out=BV[:], in_=psI[:, 0:E]),
                              waits=[("tS", t1)], inc="vS")
                t2 = sch.emit("tensor", lambda e: [
                    e.matmul(psI[:, nh * 512:(nh + 1) * 512], ones_f[:],
                             bo1[:, nh * 512:(nh + 1) * 512], start=True, stop=True)
                    for nh in range(2)][-1],
                              waits=[("vS", v1)], inc="tS")
                v2 = sch.emit("vector", lambda e: e.tensor_copy(out=BO[:], in_=psI[:]),
                              waits=[("tS", t2)], inc="vS")

                def t_b1e(e):
                    _l = None
                    for mh in range(EC):
                        for c in range(2):
                            e.matmul(psBE[:, mh, :], w1f[:, c, mh * 128:(mh + 1) * 128],
                                     braw["cb"][:, c, :], start=(c == 0), stop=False)
                        for c in range(2):
                            _l = e.matmul(psBE[:, mh, :], w1f[:, 2 + c, mh * 128:(mh + 1) * 128],
                                          braw["bp"][:, c, :], start=False, stop=(c == 1))
                    return _l
                t3 = sch.emit("tensor", t_b1e, waits=[("ld", lw1f), ("vS", vc)], inc="tS")
                sch.emit("vector", lambda e: e.tensor_tensor(out=bias["b1e"][:], in0=psBE[:],
                                                             in1=braw["b1"][:], op=OP.add),
                         waits=[("tS", t3)], inc="vS")
                sch.barrier()

            def bcast_scal(wait):
                t = sch.emit("tensor", lambda e: e.matmul(psBC[:], ones_f[:], srow[:],
                                                          start=True, stop=True),
                             waits=[wait], inc="tS")
                v = sch.emit("vector", lambda e: e.tensor_copy(out=SCAL[:], in_=psBC[:]),
                             waits=[("tS", t)], inc="vS")
                return ("vS", v)

            def collective(k, ready_wait):
                g1 = sch.emit("gpsimd", lambda e, k=k: e.dma_start(out=cc_in[k].ap(), in_=ccs[:]),
                              waits=[ready_wait], inc="gd", amt=16)
                sch.emit("gpsimd", lambda e, k=k: e.collective_compute(
                    "AllReduce", OP.max, replica_groups=[list(range(NCORES))],
                    ins=[cc_in[k].ap().opt()], outs=[cc_out[k].ap().opt()]),
                    waits=[("gd", g1)], inc="cc")
                g3 = sch.emit("gpsimd", lambda e, k=k: e.dma_start(out=ccs[:], in_=cc_out[k].ap()),
                              waits=[("cc", sch.val("cc"))], inc="gd", amt=16)
                return ("gd", g3)

            def fold_to_scalar(pm_ap, wait, dstcol):
                t = sch.emit("tensor", lambda e: e.transpose(psXP[:], pm_ap, ident32[:]),
                             waits=[wait], inc="tS")
                v = sch.emit("vector", lambda e, c=dstcol: e.tensor_reduce(
                    out=ccs[0:1, c:c + 1], in_=psXP[:], axis=AX, op=OP.max),
                    waits=[("tS", t)], inc="vS")
                return ("vS", v)

            cut(2)
            with contextlib.ExitStack() as esA:
                EA = esA.enter_context
                xfm_g = EA(nc.sbuf_tensor("xfm_g", [128, 2, DC, 512], BF16))
                h_t = EA(nc.sbuf_tensor("h_t", [128, EC, S + 4], F32))
                h_bf = EA(nc.sbuf_tensor("h_bf", [128, EC, S + 4], BF16))
                q4 = EA(nc.sbuf_tensor("q4", [128, EC, S], BF16))
                k4 = EA(nc.sbuf_tensor("k4", [128, EC, S], BF16))
                vtm = EA(nc.sbuf_tensor("vtm", [128, NT, E], BF16))
                pb_bf = EA(nc.sbuf_tensor("pb_bf", [128, EC, S], BF16))
                summ_bf = EA(nc.sbuf_tensor("summ_bf", [128, EC, S], BF16))
                u_bf = EA(nc.sbuf_tensor("u_bf", [128, EC, S], BF16))
                gate_bc = EA(nc.sbuf_tensor("gate_bc", [128, S], F32))
                gate_row = EA(nc.sbuf_tensor("gate_row", [1, 512], F32))
                xtmb = EA(nc.sbuf_tensor("xtmb", [128, 2, D], F32))
                xbf_t = EA(nc.sbuf_tensor("xbf_t", [128, 2, D], BF16))
                trbuf = EA(nc.sbuf_tensor("trbuf", [128, 1024], F32))
                ph_bf = EA(nc.sbuf_tensor("ph_bf", [128, EC, 128], BF16))
                pbc_bf = EA(nc.sbuf_tensor("pbc_bf", [128, E], BF16))
                scb = EA(nc.sbuf_tensor("scb", [128, NT, 192], F32))
                e0b = EA(nc.sbuf_tensor("e0b", [128, NT, 192], F32))
                t8b = EA(nc.sbuf_tensor("t8b", [128, NT, 8], F32))
                negmb = EA(nc.sbuf_tensor("negmb", [128, NT], F32))
                rsum = EA(nc.sbuf_tensor("rsum", [128, NT], F32))
                rinvb = EA(nc.sbuf_tensor("rinvb", [128, NT], F32))
                wT_bf = EA(nc.sbuf_tensor("wT_bf", [128, 2, 192], BF16))
                wtA = EA(nc.sbuf_tensor("wtA", [128, NT, 128], BF16))
                wtB = EA(nc.sbuf_tensor("wtB", [128, NT, 128], BF16))
                dtile = EA(nc.sbuf_tensor("dtile", [128, 2, 512], F32))
                ln1buf = EA(nc.sbuf_tensor("ln1buf", [128, 4, D], F32))
                bnb = EA(nc.sbuf_tensor("bnb", [128, 4, 2, 6], F32))
                mvb = EA(nc.sbuf_tensor("mvb", [128, 4, 2, 2], F32))
                lnsc = EA(nc.sbuf_tensor("lnsc", [128, 4, 6], F32))
                meanb = EA(nc.sbuf_tensor("meanb", [128, 4], F32))
                varb = EA(nc.sbuf_tensor("varb", [128, 4], F32))
                rstdb = EA(nc.sbuf_tensor("rstdb", [128, 4], F32))
                sxb = EA(nc.sbuf_tensor("sxb", [128, 32], F32))
                xres = xtmb

                for b in range(B):
                    if b == 1:
                        cut(35)
                    with contextlib.ExitStack() as es01:
                        E01 = es01.enter_context
                        psXT = [E01(nc.psum_tensor(un(f"psXT{i}"), [128, DC, 128], BF16))
                                for i in range(2)]
                        psH = E01(nc.psum_tensor(un("psH"), [128, EC, 512], F32))
                        xread_free = [None, None]
                        xbf_free = [None, None]
                        psxt_free = [None, None]
                        h0_free = [None]
                        sch.emit("vector", lambda e: [e.memset(h_t[:, :, 0:4], 0.0),
                                                      e.memset(h_bf[:, :, 0:4], 0.0)][-1], inc="vS")
                        xfmg_free = [None, None]
                        for g in range(4):
                            gp = g % 2
                            last_ev = None
                            for i in range(4):
                                tt = 4 * g + i
                                pid = tt % 2
                                lsem = "xA" if pid == 0 else "xB"
                                lv = sch.emit("sync", lambda e, b=b, tt=tt, pid=pid: e.dma_start(
                                    out=xtmb[:, pid, :], in_=d_x[b, tt * 128:(tt + 1) * 128, :]),
                                    waits=[xread_free[pid]], inc=lsem, amt=16)
                                cv = sch.emit("vector", lambda e, pid=pid: e.tensor_copy(
                                    out=xbf_t[:, pid, :], in_=xtmb[:, pid, :]),
                                    waits=[(lsem, lv), xbf_free[pid]], inc="vS")
                                tv = sch.emit("tensor", lambda e, pid=pid: [
                                    e.transpose(psXT[pid][:, c, :],
                                                xbf_t[:, pid, c * 128:(c + 1) * 128], ident_bf[:])
                                    for c in range(DC)][-1],
                                    waits=[("vS", cv), psxt_free[pid]], inc="tS")
                                ev = sch.emit("vector", lambda e, i=i, gp=gp, pid=pid: e.tensor_copy(
                                    out=xfm_g[:, gp, :, i * 128:(i + 1) * 128], in_=psXT[pid][:]),
                                    waits=[("tS", tv), xfmg_free[gp]], inc="vS")
                                xread_free[pid] = ("vS", cv)
                                xbf_free[pid] = ("tS", tv)
                                psxt_free[pid] = ("vS", ev)
                                last_ev = ev
                            tm = sch.emit("tensor", lambda e, gp=gp: [
                                e.matmul(psH[:, mh, :], wi_bf[:, c, mh * 128:(mh + 1) * 128],
                                         xfm_g[:, gp, c, :],
                                         start=(c == 0), stop=(c == DC - 1))
                                for mh in range(EC) for c in range(DC)][-1],
                                waits=[("vS", last_ev), h0_free[0]], inc="tS")
                            xfmg_free[gp] = ("tS", tm)
                            sv = sch.emit("scalar", lambda e, g=g: [
                                e.activation(out=h_t[:, mh, 4 + g * 512: 4 + (g + 1) * 512],
                                             in_=psH[:, mh, :], func=ACT.Identity,
                                             bias=bias["bi"][:, mh, :], scale=1.0)
                                for mh in range(EC)][-1],
                                waits=[("tS", tm)], inc="sS")
                            h0_free[0] = ("sS", sv)
                        sch.barrier()

                    cut(3)
                    for st in range(2):
                        hc = sch.emit("vector", lambda e: [
                            e.tensor_copy(out=h_bf[:, c, :], in_=h_t[:, c, :])
                            for c in range(EC)][-1], inc="vS")
                        cut(19)
                        with contextlib.ExitStack() as esp:
                            psPB = esp.enter_context(nc.psum_tensor(un("psPB"), [128, E], F32))
                            psPBE = esp.enter_context(nc.psum_tensor(un("psPBE"), [128, 512], F32))
                            tv = sch.emit("vector", lambda e: [x for c in range(EC) for x in [
                                e.tensor_tensor(out=trbuf[:, 0:1024], in0=h_t[:, c, 4:2052:2],
                                                in1=h_t[:, c, 5:2052:2], op=OP.add),
                                e.tensor_tensor(out=trbuf[:, 0:512], in0=trbuf[:, 0:1024:2],
                                                in1=trbuf[:, 1:1024:2], op=OP.add),
                                e.tensor_tensor(out=trbuf[:, 0:256], in0=trbuf[:, 0:512:2],
                                                in1=trbuf[:, 1:512:2], op=OP.add),
                                e.tensor_tensor(out=trbuf[:, 0:128], in0=trbuf[:, 0:256:2],
                                                in1=trbuf[:, 1:256:2], op=OP.add),
                                e.tensor_scalar(out=ph_bf[:, c, :], in0=trbuf[:, 0:128],
                                                scalar1=1.0 / 16.0, scalar2=None, op0=OP.mult)]][-1],
                                inc="vS")
                            tp = sch.emit("tensor", lambda e: [
                                e.matmul(psPB[:], ph_bf[:, c, :], wp_bf[:, c, :],
                                         start=(c == 0), stop=(c == EC - 1)) for c in range(EC)][-1],
                                waits=[("vS", tv)], inc="tS")
                            pv = sch.emit("vector", lambda e: e.tensor_copy(out=pbc_bf[:], in_=psPB[:]),
                                          waits=[("tS", tp)], inc="vS")
                            cut(20)
                            pbe_free = [None]
                            for mh in range(EC):
                                for ng in range(4):
                                    te = sch.emit("tensor", lambda e, mh=mh, ng=ng: e.matmul(
                                        psPBE[:], pbc_bf[:, mh * 128:(mh + 1) * 128],
                                        r16_bf[:, ng * 512:(ng + 1) * 512], start=True, stop=True),
                                        waits=[("vS", pv), pbe_free[0]], inc="tS")
                                    se = sch.emit("scalar", lambda e, mh=mh, ng=ng: e.copy(
                                        out=pb_bf[:, mh, ng * 512:(ng + 1) * 512], in_=psPBE[:]),
                                        waits=[("tS", te)], inc="sS")
                                    pbe_free[0] = ("sS", se)
                            sch.barrier()

                        cut(21)
                        with contextlib.ExitStack() as esq:
                            psQK = [esq.enter_context(nc.psum_tensor(un(f"psQK{i}"), [128, 512], F32))
                                    for i in range(2)]
                            psV = [esq.enter_context(nc.psum_tensor(un(f"psV{i}"), [128, E], F32))
                                   for i in range(2)]
                            qk_free = [None, None]
                            v_free = [None, None]
                            for (wt, dst, bnm) in ((wq_bf, q4, "bq4"), (wk_bf, k4, "bk4")):
                                for mh in range(EC):
                                    for ng in range(4):
                                        pid = (mh * 4 + ng) % 2
                                        tq = sch.emit("tensor",
                                                      lambda e, wt=wt, mh=mh, ng=ng, pid=pid: [
                                                          e.matmul(psQK[pid][:],
                                                                   wt[:, c, mh * 128:(mh + 1) * 128],
                                                                   h_bf[:, c, 4 + ng * 512:
                                                                        4 + (ng + 1) * 512],
                                                                   start=(c == 0), stop=(c == EC - 1))
                                                          for c in range(EC)][-1],
                                                      waits=[("vS", hc), qk_free[pid]], inc="tS")
                                        sq = sch.emit("scalar",
                                                      lambda e, dst=dst, mh=mh, ng=ng, pid=pid,
                                                      bnm=bnm: e.activation(
                                                          out=dst[:, mh, ng * 512:(ng + 1) * 512],
                                                          in_=psQK[pid][:], func=ACT.Identity,
                                                          bias=bias[bnm][:, mh, :], scale=0.25),
                                                      waits=[("tS", tq)], inc="sS")
                                        qk_free[pid] = ("sS", sq)
                            for tt in range(NT):
                                pid = tt % 2
                                tvv = sch.emit("tensor", lambda e, tt=tt, pid=pid: [
                                    e.matmul(psV[pid][:],
                                             h_bf[:, c, 4 + tt * 128: 4 + (tt + 1) * 128],
                                             wv_bf[:, c, :], start=(c == 0), stop=(c == EC - 1))
                                    for c in range(EC)][-1],
                                    waits=[("vS", hc), v_free[pid]], inc="tS")
                                vv = sch.emit("vector", lambda e, tt=tt, pid=pid: e.tensor_tensor(
                                    out=vtm[:, tt, :], in0=psV[pid][:], in1=BV[:], op=OP.add),
                                    waits=[("tS", tvv)], inc="vS")
                                v_free[pid] = ("vS", vv)
                            sch.barrier()

                        cut(22)
                        with contextlib.ExitStack() as ess:
                            psSC = [ess.enter_context(nc.psum_tensor(un(f"psSC{i}"), [128, 192], F32))
                                    for i in range(2)]
                            psWT = [ess.enter_context(nc.psum_tensor(un("psWT"), [128, 192], BF16))
                                    for _ in range(2)]
                            psSM = [ess.enter_context(nc.psum_tensor(un("psSM"), [128, EC, 128], F32))
                                    for _ in range(2)]
                            sc_free = [None, None]
                            wt_free = [None, None]
                            sm_free = [None, None]
                            for tt in range(NT):
                                pid = tt % 2
                                if tt == 0:
                                    mz = sch.emit("vector", lambda e, pid=pid: e.memset(
                                        psSC[pid][:, 0:64], 0.0), waits=[sc_free[pid]], inc="vS")
                                    tw = [("vS", mz)]
                                    fn = lambda e, pid=pid: [
                                        e.matmul(psSC[pid][:, 64:192], q4[:, c, 0:128],
                                                 k4[:, c, 0:128], start=(c == 0), stop=(c == EC - 1))
                                        for c in range(EC)][-1]
                                else:
                                    tw = [sc_free[pid]]
                                    fn = lambda e, tt=tt, pid=pid: [
                                        e.matmul(psSC[pid][:], q4[:, c, tt * 128:(tt + 1) * 128],
                                                 k4[:, c, tt * 128 - 64:tt * 128 + 128],
                                                 start=(c == 0), stop=(c == EC - 1))
                                        for c in range(EC)][-1]
                                tsc = sch.emit("tensor", fn, waits=tw, inc="tS")
                                vsc = sch.emit("vector", lambda e, tt=tt, pid=pid: e.tensor_tensor(
                                    out=scb[:, tt, :], in0=psSC[pid][:],
                                    in1=(MASK0 if tt == 0 else MASK)[:], op=OP.add),
                                    waits=[("tS", tsc)], inc="vS")
                                sc_free[pid] = ("vS", vsc)
                                sch.emit("vector", lambda e, tt=tt: e.max(
                                    out=t8b[:, tt, :], in_=scb[:, tt, :]), inc="vS")
                            sch.emit("vector", lambda e: e.drain(), inc="vS")
                            dr2 = sch.emit("vector", lambda e: e.tensor_scalar(
                                out=negmb[:], in0=t8b[:, :, 0], scalar1=-1.0, scalar2=None,
                                op0=OP.mult), inc="vS")
                            dr2 = sch.emit("vector", lambda e: e.drain(), inc="vS")
                            exps = []
                            for tt in range(NT):
                                sexp = sch.emit("scalar", lambda e, tt=tt: e.activation(
                                    out=e0b[:, tt, :], in_=scb[:, tt, :], func=ACT.Exp,
                                    bias=negmb[:, tt:tt + 1], scale=1.0),
                                    waits=[("vS", dr2)], inc="sS")
                                exps.append(sexp)
                            for tt in range(NT):
                                sch.emit("vector", lambda e, tt=tt: e.tensor_scalar(
                                    out=scb[:, tt, :], in0=scb[:, tt, :], scalar1=t8b[:, tt, 7:8],
                                    scalar2=None, op0=OP.is_ge),
                                    waits=[("sS", exps[tt])], inc="vS")
                            ee_ = sch.emit("vector", lambda e: e.tensor_tensor(
                                out=e0b[:].rearrange("p t j -> p (t j)"),
                                in0=e0b[:].rearrange("p t j -> p (t j)"),
                                in1=scb[:].rearrange("p t j -> p (t j)"), op=OP.mult), inc="vS")
                            sch.emit("vector", lambda e: e.memset(e0b[0:1, 0, :], 0.0), inc="vS")
                            sch.emit("vector", lambda e: e.tensor_reduce(
                                out=rsum[:, :, None], in_=e0b[:], axis=AX, op=OP.add), inc="vS")
                            sch.emit("vector", lambda e: e.drain(), inc="vS")
                            sch.emit("vector", lambda e: e.tensor_scalar(
                                out=rsum[:], in0=rsum[:], scalar1=1e-30, scalar2=None, op0=OP.max),
                                inc="vS")
                            sch.emit("vector", lambda e: e.drain(), inc="vS")
                            sch.emit("vector", lambda e: e.reciprocal(out=rinvb[:], in_=rsum[:]),
                                     inc="vS")
                            sch.emit("vector", lambda e: e.drain(), inc="vS")
                            wTf = [None, None]
                            for tt in range(NT):
                                wp_ = tt % 2
                                wv_ = sch.emit("vector", lambda e, tt=tt, wp_=wp_: e.tensor_scalar(
                                    out=wT_bf[:, wp_, :], in0=e0b[:, tt, :],
                                    scalar1=rinvb[:, tt:tt + 1], scalar2=None, op0=OP.mult),
                                    waits=[wTf[wp_]], inc="vS")
                                twt = sch.emit("tensor", lambda e, wp_=wp_: [
                                    e.transpose(psWT[wp_][64:128, 0:128], wT_bf[:, wp_, 0:64],
                                                ident_bf[:]),
                                    e.transpose(psWT[wp_][:, 64:192], wT_bf[:, wp_, 64:192],
                                                ident_bf[:])][-1],
                                    waits=[("vS", wv_), wt_free[wp_]], inc="tS")
                                wTf[wp_] = ("tS", twt)
                                vwt = sch.emit("vector", lambda e, tt=tt, wp_=wp_: [
                                    e.tensor_copy(out=wtA[64:128, tt, :],
                                                  in_=psWT[wp_][64:128, 0:128]),
                                    e.tensor_copy(out=wtB[:, tt, :], in_=psWT[wp_][:, 64:192])][-1],
                                    waits=[("tS", twt)], inc="vS")
                                wt_free[wp_] = ("vS", vwt)
                                tsm = sch.emit("tensor", lambda e, tt=tt, wp_=wp_: [
                                    x for mh in range(EC) for x in
                                    ([e.matmul(psSM[wp_][:, mh, :],
                                               vtm[64:128, tt - 1, mh * 128:(mh + 1) * 128],
                                               wtA[64:128, tt, :], start=True, stop=False)]
                                     if tt > 0 else []) +
                                    [e.matmul(psSM[wp_][:, mh, :],
                                              vtm[:, tt, mh * 128:(mh + 1) * 128],
                                              wtB[:, tt, :], start=(tt == 0), stop=True)]][-1],
                                    waits=[("vS", vwt), sm_free[wp_]], inc="tS")
                                vsm = sch.emit("vector", lambda e, tt=tt, wp_=wp_: e.tensor_copy(
                                    out=summ_bf[:, :, tt * 128:(tt + 1) * 128], in_=psSM[wp_][:]),
                                    waits=[("tS", tsm)], inc="vS")
                                sm_free[wp_] = ("vS", vsm)
                            sch.barrier()

                        cut(23)
                        with contextlib.ExitStack() as esg:
                            psG = esg.enter_context(nc.psum_tensor(un("psG"), [1, 512], F32))
                            psGB = [esg.enter_context(nc.psum_tensor(un(f"psGB{i}"), [128, 512], F32))
                                    for i in range(2)]
                            g_free = [None]
                            grow_free = [None]
                            gb_free = [None, None]
                            for ng in range(4):
                                tg = sch.emit("tensor", lambda e, ng=ng: [
                                    e.matmul(psG[:], wh2_bf[:, c, :],
                                             h_bf[:, c, 4 + ng * 512: 4 + (ng + 1) * 512],
                                             start=(c == 0), stop=(c == EC - 1))
                                    for c in range(EC)][-1],
                                    waits=[g_free[0]], inc="tS")
                                sg = sch.emit("scalar", lambda e, ng=ng: e.activation(
                                    out=gate_row[:], in_=psG[:],
                                    func=ACT.Sigmoid, bias=bh2_s[:], scale=1.0),
                                    waits=[("tS", tg), grow_free[0]], inc="sS")
                                g_free[0] = ("sS", sg)
                                tgb = sch.emit("tensor", lambda e, ng=ng: e.matmul(
                                    psGB[ng % 2][:], ones_f[:], gate_row[:],
                                    start=True, stop=True),
                                    waits=[("sS", sg), gb_free[ng % 2]], inc="tS")
                                vgb = sch.emit("vector", lambda e, ng=ng: e.tensor_copy(
                                    out=gate_bc[:, ng * 512:(ng + 1) * 512], in_=psGB[ng % 2][:]),
                                    waits=[("tS", tgb)], inc="vS")
                                gb_free[ng % 2] = ("vS", vgb)
                                grow_free[0] = ("tS", tgb)
                            sch.barrier()

                        cut(24)
                        with contextlib.ExitStack() as esd:
                            psD1 = [esd.enter_context(nc.psum_tensor(un(f"psD1{i}"), [128, 512], F32))
                                    for i in range(2)]
                            psD2 = [esd.enter_context(nc.psum_tensor(un(f"psD2{i}"), [128, 512], F32))
                                    for i in range(2)]
                            d1_free = [None, None]
                            for mh in range(EC):
                                for ng in range(4):
                                    pid = (mh * 4 + ng) % 2
                                    def w1mm(e, mh=mh, ng=ng, pid=pid):
                                        sl = slice(4 + ng * 512, 4 + (ng + 1) * 512)
                                        for c in range(EC):
                                            e.matmul(psD1[pid][:],
                                                     w1_bf[:, c, mh * 128:(mh + 1) * 128],
                                                     h_bf[:, c, sl], start=(c == 0), stop=False)
                                        for k in range(5):
                                            for c in range(EC):
                                                e.matmul(psD1[pid][:],
                                                         w1k_bf[:, k, c, mh * 128:(mh + 1) * 128],
                                                         h_bf[:, c, ng * 512 + k:
                                                              ng * 512 + k + 512],
                                                         start=False, stop=False)
                                        for c in range(EC):
                                            e.matmul(psD1[pid][:],
                                                     w1_bf[:, 4 + c, mh * 128:(mh + 1) * 128],
                                                     pb_bf[:, c, ng * 512:(ng + 1) * 512],
                                                     start=False, stop=False)
                                        _l = None
                                        for c in range(EC):
                                            _l = e.matmul(psD1[pid][:],
                                                          w1_bf[:, 6 + c,
                                                                mh * 128:(mh + 1) * 128],
                                                          summ_bf[:, c,
                                                                  ng * 512:(ng + 1) * 512],
                                                          start=False, stop=(c == EC - 1))
                                        return _l
                                    t1m = sch.emit("tensor", w1mm, waits=[d1_free[pid]], inc="tS")
                                    s1m = sch.emit("scalar", lambda e, mh=mh, ng=ng, pid=pid:
                                                   e.activation(
                                                       out=u_bf[:, mh, ng * 512:(ng + 1) * 512],
                                                       in_=psD1[pid][:], func=ACT.Tanh,
                                                       bias=bias["b1e"][:, mh, :], scale=1.0),
                                                   waits=[("tS", t1m)], inc="sS")
                                    d1_free[pid] = ("sS", s1m)
                            cut(24.3)
                            udone = ("sS", sch.val("sS"))
                            d2_free = [None, None]
                            dtile_free = [None, None]
                            for mh in range(EC):
                                for ng in range(4):
                                    pid = (mh * 4 + ng) % 2
                                    t2m = sch.emit("tensor", lambda e, mh=mh, ng=ng, pid=pid: [
                                        e.matmul(psD2[pid][:], w2_bf[:, c, mh * 128:(mh + 1) * 128],
                                                 u_bf[:, c, ng * 512:(ng + 1) * 512],
                                                 start=(c == 0), stop=(c == EC - 1))
                                        for c in range(EC)][-1],
                                        waits=[udone, d2_free[pid]], inc="tS")
                                    s2m = sch.emit("scalar", lambda e, mh=mh, pid=pid: e.activation(
                                        out=dtile[:, pid, :], in_=psD2[pid][:], func=ACT.Identity,
                                        bias=bias["b2"][:, mh, :], scale=1.0),
                                        waits=[("tS", t2m), dtile_free[pid]], inc="sS")
                                    v2m = sch.emit("vector", lambda e, ng=ng, pid=pid: e.tensor_tensor(
                                        out=dtile[:, pid, :], in0=dtile[:, pid, :],
                                        in1=gate_bc[:, ng * 512:(ng + 1) * 512], op=OP.mult),
                                        waits=[("sS", s2m)], inc="vS")
                                    v3m = sch.emit("vector", lambda e, mh=mh, ng=ng, pid=pid:
                                                   e.tensor_tensor(
                                        out=h_t[:, mh, 4 + ng * 512: 4 + (ng + 1) * 512],
                                        in0=h_t[:, mh, 4 + ng * 512: 4 + (ng + 1) * 512],
                                        in1=dtile[:, pid, :], op=OP.add), inc="vS")
                                    d2_free[pid] = ("sS", s2m)
                                    dtile_free[pid] = ("vS", v3m)
                            sch.barrier()
                    cut(25)
                    hcf = sch.emit("vector", lambda e: [
                        e.tensor_copy(out=h_bf[:, c, :], in_=h_t[:, c, :])
                        for c in range(EC)][-1], inc="vS")
                    if dbg:
                        sch.emit("gpsimd", lambda e, b=b: e.dma_start(
                            out=d_dbg_h[b], in_=h_t[:, :, 4:2052]), waits=[("vS", hcf)],
                            inc="gd", amt=16)
                    with contextlib.ExitStack() as es3:
                        psAO = [es3.enter_context(nc.psum_tensor(un(f"psAO{i}"), [128, 2, 512], F32))
                                for i in range(2)]
                        ao_free = [None, None]
                        xr_free = [None, None]
                        for hb in range(4):
                            for ti in range(4):
                                tt = hb * 4 + ti
                                tt32 = b * 16 + tt
                                pid = tt % 2
                                lsem = "wA" if pid == 0 else "wB"
                                lx = sch.emit("sync", lambda e, b=b, tt=tt, pid=pid: e.dma_start(
                                    out=xres[:, pid, :], in_=d_x[b, tt * 128:(tt + 1) * 128, :]),
                                    waits=[xr_free[pid]], inc=lsem, amt=16)
                                tao = sch.emit("tensor", lambda e, tt=tt, pid=pid: [
                                    e.matmul(psAO[pid][:, nh, :],
                                             h_bf[:, c, 4 + tt * 128: 4 + (tt + 1) * 128],
                                             wo_bf[:, c, nh * 512:(nh + 1) * 512],
                                             start=(c == 0), stop=(c == EC - 1))
                                    for nh in range(2) for c in range(EC)][-1],
                                    waits=[("vS", hcf), ao_free[pid]], inc="tS")
                                vao = sch.emit("vector", lambda e, ti=ti, pid=pid: [
                                    e.scalar_tensor_tensor(
                                        out=ln1buf[:, ti, nh * 512:(nh + 1) * 512],
                                        in0=xres[:, pid, nh * 512:(nh + 1) * 512], scalar=2.0,
                                        in1=psAO[pid][:, nh, :], op0=OP.mult, op1=OP.add)
                                    for nh in range(2)][-1],
                                    waits=[(lsem, lx), ("tS", tao), ("gd", sch.val("gd"))],
                                    inc="vS")
                                sch.emit("vector", lambda e, ti=ti: e.tensor_tensor(
                                    out=ln1buf[:, ti, :], in0=ln1buf[:, ti, :], in1=BO[:],
                                    op=OP.add), inc="vS")
                                ao_free[pid] = ("vS", vao)
                                xr_free[pid] = ("vS", vao)
                                sch.emit("vector", lambda e, ti=ti: [
                                    e.bn_stats(out=bnb[:, ti, 0, :], in_=ln1buf[:, ti, 0:512]),
                                    e.bn_stats(out=bnb[:, ti, 1, :], in_=ln1buf[:, ti, 512:1024])][-1],
                                    inc="vS")
                            def combine(e):
                                for ti in range(4):
                                    e.bn_aggr(out=mvb[:, ti, 0, :], in_=bnb[:, ti, 0, :])
                                    e.bn_aggr(out=mvb[:, ti, 1, :], in_=bnb[:, ti, 1, :])
                                e.drain()
                                e.tensor_tensor(out=lnsc[:, :, 0], in0=mvb[:, :, 0, 0],
                                                in1=mvb[:, :, 1, 0], op=OP.subtract)
                                e.tensor_tensor(out=lnsc[:, :, 2], in0=mvb[:, :, 0, 0],
                                                in1=mvb[:, :, 1, 0], op=OP.add)
                                e.tensor_tensor(out=lnsc[:, :, 3], in0=mvb[:, :, 0, 1],
                                                in1=mvb[:, :, 1, 1], op=OP.add)
                                e.drain()
                                e.tensor_scalar(out=lnsc[:, :, 1], in0=lnsc[:, :, 0], scalar1=0.5,
                                                scalar2=None, op0=OP.mult)
                                e.tensor_scalar(out=meanb[:], in0=lnsc[:, :, 2], scalar1=0.5,
                                                scalar2=None, op0=OP.mult)
                                e.drain()
                                e.tensor_tensor(out=lnsc[:, :, 4], in0=lnsc[:, :, 1],
                                                in1=lnsc[:, :, 1], op=OP.mult)
                                e.drain()
                                return e.scalar_tensor_tensor(
                                    out=varb[:], in0=lnsc[:, :, 3], scalar=0.5,
                                    in1=lnsc[:, :, 4], op0=OP.mult, op1=OP.add)
                            cv = sch.emit("vector", combine, inc="vS")
                            sqv = sch.emit("scalar", lambda e: e.activation(
                                out=varb[:], in_=varb[:], func=ACT.Sqrt, bias=epsT[:], scale=1.0),
                                waits=[("vS", cv)], inc="sS")
                            sch.emit("vector", lambda e: e.reciprocal(out=rstdb[:], in_=varb[:]),
                                     waits=[("sS", sqv)], inc="vS")
                            drv = sch.emit("vector", lambda e: e.drain(), inc="vS")
                            for ti in range(4):
                                tt32 = b * 16 + hb * 4 + ti
                                nv = sch.emit("vector", lambda e, ti=ti: e.tensor_scalar(
                                    out=ln1buf[:, ti, :], in0=ln1buf[:, ti, :],
                                    scalar1=meanb[:, ti:ti + 1], scalar2=rstdb[:, ti:ti + 1],
                                    op0=OP.subtract, op1=OP.mult), waits=[("vS", drv)], inc="vS")
                                sch.emit("vector", lambda e, ti=ti, tt32=tt32: e.tensor_reduce(
                                    out=sxb[:, tt32:tt32 + 1], in_=ln1buf[:, ti, :], axis=AX,
                                    op=OP.max, apply_absolute_value=True), inc="vS")
                                sch.emit("gpsimd", lambda e, ti=ti, tt32=tt32: e.dma_start(
                                    out=d_ln1[tt32], in_=ln1buf[:, ti, :]),
                                    waits=[("vS", sch.val("vS"))], inc="gd", amt=16)
                                if dbg:
                                    sch.emit("gpsimd", lambda e, ti=ti, tt32=tt32: e.dma_start(
                                        out=d_dbg_ln1[tt32], in_=ln1buf[:, ti, :]),
                                        waits=[("vS", sch.val("vS"))], inc="gd", amt=16)
                        sch.barrier()
                ln1_sd = ("gd", sch.val("gd"))

                cut(40)
                f1 = sch.emit("vector", lambda e: e.tensor_reduce(
                    out=foldt[:], in_=sxb[:], axis=AX, op=OP.max), inc="vS")
                f1v = ("vS", sch.val("vS"))
                fw = fold_to_scalar(foldt[:], ("vS", f1), 0)
                b1w = collective(0, fw)
                def d1(e):
                    e.tensor_scalar(out=srow[:, 1:2], in0=ccs[0:1, 0:1], scalar1=1.0 / 32767.0,
                                    scalar2=1e-12, op0=OP.mult, op1=OP.max)
                    e.drain()
                    e.reciprocal(out=srow[:, 0:1], in_=srow[:, 1:2])
                    return e.drain()
                dv1 = sch.emit("vector", d1, waits=[b1w], inc="vS")
                bc1 = bcast_scal(("vS", dv1))

            gr_sp_done = [None] * ICN
            gp_sp_done = [None] * ICN
            wub_sp_done = [None] * ICN
            cut(45)
            with contextlib.ExitStack() as esF:
                EF = esF.enter_context
                xr_fm = EF(nc.sbuf_tensor("xr_fm", [128, DC, TPC], BF16))
                lbuf = EF(nc.sbuf_tensor("lbuf", [128, 2, D], F32))
                xrb_t = EF(nc.sbuf_tensor("xrb_t", [128, 2, D], BF16))
                wraw = EF(nc.sbuf_tensor("wraw", [128, 2, D], F32))
                bmax = EF(nc.sbuf_tensor("bmax", [128, 2, 32], F32))
                binv = EF(nc.sbuf_tensor("binv", [128, 2, 32], F32))
                wq1 = EF(nc.sbuf_tensor("wq1", [128, D], F32))
                wqb = EF(nc.sbuf_tensor("wqb", [128, 2, D], BF16))
                wt_bf = EF(nc.sbuf_tensor("wt_bf", [128, 2, DC, 128], BF16))
                wt_g = EF(nc.sbuf_tensor("wt_g", [128, 2, DC, 128], BF16))
                wt_u = EF(nc.sbuf_tensor("wt_u", [128, 2, DC, 128], BF16))
                gmb = EF(nc.sbuf_tensor("gmb", [128, 224], F32))
                umb = EF(nc.sbuf_tensor("umb", [128, 224], F32))
                gpmb = EF(nc.sbuf_tensor("gpmb", [128, 64], F32))
                opmb = EF(nc.sbuf_tensor("opmb", [128, 64], F32))
                HW = TPC // 2
                gbuf = EF(nc.sbuf_tensor("gbuf", [128, 2, HW], F32))
                grb2 = EF(nc.sbuf_tensor("grb2", [128, 2, HW], F32))
                sgb2 = EF(nc.sbuf_tensor("sgb2", [128, 2, HW], F32))
                gpb = EF(nc.sbuf_tensor("gpb", [128, 2, HW], F32))
                psWT2 = [EF(nc.psum_tensor(un(f"psWT2{i}"), [128, DC, 128], BF16)) for i in range(2)]
                psUG = [EF(nc.psum_tensor(un(f"psUG{i}"), [128, 512], F32)) for i in range(4)]

                psXR = psWT2
                lb_free = [None, None]
                xrps_free = [None, None]
                xrb_free = [None, None]
                for t32 in range(NT32):
                    pid = t32 % 2
                    lsem = "xA" if pid == 0 else "xB"
                    lv = sch.emit("sync", lambda e, t32=t32, pid=pid: e.dma_start(
                        out=lbuf[:, pid, :], in_=d_ln1[t32]),
                        waits=[lb_free[pid], ln1_sd, f1v if t32 == 0 else None],
                        inc=lsem, amt=16)
                    va = sch.emit("vector", lambda e, pid=pid: e.tensor_scalar(
                        out=lbuf[:, pid, :], in0=lbuf[:, pid, :], scalar1=SCAL[:, 0:1],
                        scalar2=MAGIC, op0=OP.mult, op1=OP.add),
                        waits=[(lsem, lv), bc1], inc="vS")
                    vb = sch.emit("vector", lambda e, pid=pid: [
                        e.drain(),
                        e.tensor_scalar(
                            out=xrb_t[:, pid, :], in0=lbuf[:, pid, :], scalar1=MAGIC,
                            scalar2=None, op0=OP.subtract)][-1],
                        waits=[xrb_free[pid]], inc="vS")
                    lb_free[pid] = ("vS", vb)
                    tx = sch.emit("tensor", lambda e, pid=pid: [
                        e.transpose(psXR[pid][:, c, :], xrb_t[:, pid, c * 128:(c + 1) * 128],
                                    ident_bf[:]) for c in range(DC)][-1],
                        waits=[("vS", vb), xrps_free[pid]], inc="tS")
                    xrb_free[pid] = ("tS", tx)
                    vx = sch.emit("vector", lambda e, t32=t32, pid=pid: e.tensor_copy(
                        out=xr_fm[:, :, t32 * 128:(t32 + 1) * 128], in_=psXR[pid][:]),
                        waits=[("tS", tx)], inc="vS")
                    xrps_free[pid] = ("vS", vx)
                xr_done = ("vS", sch.val("vS"))
                cut(46)
                if dbg:
                    sch.emit("gpsimd", lambda e: e.dma_start(
                        out=d_dbg_xr[0], in_=xr_fm[:, :, 3584:4096]),
                        waits=[xr_done], inc="gd", amt=16)

                wraw_free = [None, None]
                wq1_free = [None]
                wtp_free = list(xrps_free)

                wqbT_free = [None, None]

                def wqdq(dsrc, pid, wslot):
                    lsem = "wA" if pid == 0 else "wB"
                    lv = sch.emit("sync", lambda e, d=dsrc, pid=pid: e.dma_start(
                        out=wraw[:, pid, :], in_=d), waits=[wraw_free[pid]], inc=lsem, amt=16)
                    def vq(e, pid=pid):
                        e.tensor_reduce(out=bmax[:, pid, :],
                                        in_=wraw[:, pid, :].rearrange("p (b k) -> p b k", k=32),
                                        axis=AX, op=OP.max, apply_absolute_value=True)
                        e.drain()
                        e.tensor_scalar(out=bmax[:, pid, :], in0=bmax[:, pid, :],
                                        scalar1=1.0 / 7.0, scalar2=1e-12, op0=OP.mult, op1=OP.max)
                        e.drain()
                        e.reciprocal(out=binv[:, pid, :], in_=bmax[:, pid, :])
                        e.drain()
                        return e.tensor_tensor(
                            out=wq1[:].rearrange("p (b k) -> p b k", k=32),
                            in0=wraw[:, pid, :].rearrange("p (b k) -> p b k", k=32),
                            in1=binv[:, pid, :, None].broadcast_to((128, 32, 32)), op=OP.mult)
                    v1 = sch.emit("vector", vq, waits=[(lsem, lv), wq1_free[0]], inc="vS")
                    s1 = sch.emit("scalar", lambda e: e.activation(
                        out=wq1[:], in_=wq1[:], func=ACT.Identity, bias=magicT[:], scale=1.0),
                        waits=[("vS", v1)], inc="sS")
                    s2 = sch.emit("scalar", lambda e: e.activation(
                        out=wq1[:], in_=wq1[:], func=ACT.Identity, bias=nmagicT[:], scale=1.0),
                        inc="sS")
                    v2 = sch.emit("vector", lambda e, pid=pid, wslot=wslot: e.tensor_tensor(
                        out=wqb[:, wslot, :].rearrange("p (b k) -> p b k", k=32),
                        in0=wq1[:].rearrange("p (b k) -> p b k", k=32),
                        in1=bmax[:, pid, :, None].broadcast_to((128, 32, 32)), op=OP.mult),
                        waits=[("sS", s2), wqbT_free[wslot]], inc="vS")
                    wraw_free[pid] = ("vS", v2)
                    wq1_free[0] = ("vS", v2)
                    return ("vS", v2)

                def wtransp(pid, wq_w):
                    tv = sch.emit("tensor", lambda e, pid=pid: [
                        e.transpose(psWT2[pid][:, c, :], wqb[:, pid, c * 128:(c + 1) * 128],
                                    ident_bf[:]) for c in range(DC)][-1],
                        waits=[wq_w, wtp_free[pid]], inc="tS")
                    vv = sch.emit("vector", lambda e, pid=pid: e.tensor_copy(
                        out=wt_bf[:, pid, :, :], in_=psWT2[pid][:]),
                        waits=[("tS", tv)], inc="vS")
                    wtp_free[pid] = ("vS", vv)
                    return ("vS", vv)

                ug_free = [None] * 4
                gsp_free = [None, None]
                wtg_free = [None, None]
                wtu_free = [None, None]
                prep_mark = {}

                wqv_mark = {}

                def prep_VS(ic):
                    pid = ic % 2
                    wqv_mark[ic] = {
                        "g": wqdq(dw["wgate"][ic], pid, 0),
                        "u": wqdq(dw["wup"][ic], pid, 1)}

                def prep_T(ic):
                    pid = ic % 2
                    marks = {}
                    for wname, wslot, dst, dfree in (("g", 0, wt_g, wtg_free),
                                                     ("u", 1, wt_u, wtu_free)):
                        tv = sch.emit("tensor", lambda e, pid=pid, wslot=wslot: [
                            e.transpose(psWT2[pid][:, c, :], wqb[:, wslot, c * 128:(c + 1) * 128],
                                        ident_bf[:]) for c in range(DC)][-1],
                            waits=[wqv_mark[ic][wname], wtp_free[pid]], inc="tS")
                        wqbT_free[wslot] = ("tS", tv)
                        vv = sch.emit("vector", lambda e, pid=pid, dst=dst: e.tensor_copy(
                            out=dst[:, pid, :, :], in_=psWT2[pid][:]),
                            waits=[("tS", tv), dfree[pid]], inc="vS")
                        wtp_free[pid] = ("vS", vv)
                        marks[wname] = ("vS", vv)
                    su = sch.emit("gpsimd", lambda e, ic=ic, pid=pid: e.dma_start(
                        out=d_wub[ic], in_=wt_u[:, pid, :, :].rearrange("p c m -> p (c m)")),
                        waits=[marks["u"]], inc="gd", amt=16)
                    wtu_free[pid] = ("gd", su)
                    wub_sp_done[ic] = ("gd", su)
                    prep_mark[ic] = marks

                prep_VS(0)
                prep_T(0)
                prep_VS(1)
                prep_T(1)
                cut(47)
                for ic in range(ICN):
                    pid = ic % 2
                    if ic == 4:
                        cut(48)
                    if ic == 12:
                        cut(48.5)
                    if ic + 2 < ICN:
                        prep_VS(ic + 2)
                    last_tm = None
                    sp = None
                    for tc in range(8):
                        bpid = tc % 4
                        tm = sch.emit("tensor", lambda e, tc=tc, pid=pid, bpid=bpid: [
                            e.matmul(psUG[bpid][:], wt_g[:, pid, c, :],
                                     xr_fm[:, c, tc * 512:(tc + 1) * 512],
                                     start=(c == 0), stop=(c == DC - 1)) for c in range(DC)][-1],
                            waits=[prep_mark[ic]["g"], xr_done] + (ug_free[bpid] or []),
                            inc="tS")
                        sv = sch.emit("scalar", lambda e, tc=tc, pid=pid, bpid=bpid: e.copy(
                            out=gpb[:, pid, (tc % 4) * 512:((tc % 4) + 1) * 512],
                            in_=psUG[bpid][:]),
                            waits=[("tS", tm), gsp_free[pid] if tc % 4 == 0 else None],
                            inc="sS")
                        vr = sch.emit("vector", lambda e, ic=ic, tc=tc, pid=pid: e.tensor_reduce(
                            out=gmb[:, ic * 8 + tc: ic * 8 + tc + 1],
                            in_=gpb[:, pid, (tc % 4) * 512:((tc % 4) + 1) * 512],
                            axis=AX, op=OP.max, apply_absolute_value=True),
                            waits=[("sS", sv)], inc="vS")
                        ug_free[bpid] = [("sS", sv)]
                        last_tm = tm
                        if tc % 4 == 3:
                            hf = tc // 4
                            sp = sch.emit("scalar", lambda e, ic=ic, hf=hf, pid=pid: e.dma_start(
                                out=d_gr[ic, :, hf * 2048:(hf + 1) * 2048], in_=gpb[:, pid, :]),
                                waits=[("vS", vr)], inc="sd", amt=16)
                            gsp_free[pid] = ("sd", sp)
                    gr_sp_done[ic] = ("sd", sp)
                    wtg_free[pid] = ("tS", last_tm)
                    if ic + 2 < ICN:
                        prep_T(ic + 2)

                cut(49)
                fg = sch.emit("vector", lambda e: e.tensor_reduce(
                    out=foldt[:], in_=gmb[:, 0:ICN * 8], axis=AX, op=OP.max), inc="vS")
                fw2 = fold_to_scalar(foldt[:], ("vS", fg), 0)
                b2w = collective(1, fw2)
                def d2(e):
                    e.tensor_scalar(out=srow[:, 3:4], in0=ccs[0:1, 0:1], scalar1=srow[:, 1:2],
                                    scalar2=1.0 / 32767.0, op0=OP.mult, op1=OP.mult)
                    e.drain()
                    e.tensor_scalar(out=srow[:, 3:4], in0=srow[:, 3:4], scalar1=1e-12,
                                    scalar2=None, op0=OP.max)
                    e.drain()
                    e.reciprocal(out=srow[:, 2:3], in_=srow[:, 3:4])
                    e.drain()
                    e.tensor_scalar(out=srow[:, 2:3], in0=srow[:, 2:3], scalar1=srow[:, 1:2],
                                    scalar2=None, op0=OP.mult)
                    return e.drain()
                dv2 = sch.emit("vector", d2, waits=[b2w], inc="vS")
                bc2 = bcast_scal(("vS", dv2))
                if dbg:
                    sch.emit("gpsimd", lambda e: e.dma_start(out=d_dbg_gum[0], in_=gmb[:, 0:192]),
                             waits=[("vS", sch.val("vS"))], inc="gd", amt=16)
                    sch.emit("gpsimd", lambda e: e.dma_start(
                        out=d_dbg_xr[2], in_=xr_fm[:, :, 3584:4096]),
                        waits=[("vS", sch.val("vS"))], inc="gd", amt=16)
                    sch.emit("gpsimd", lambda e: e.dma_start(out=d_dbg_s2, in_=srow[:]),
                             waits=[("vS", sch.val("vS"))], inc="gd", amt=16)
                sch.barrier()

                cut(50)
                wub_free = [("sd", sch.val("sd")), ("sd", sch.val("sd"))]
                gb_free = [None, None]
                gp_free = [("sd", sch.val("sd")), ("sd", sch.val("sd"))]
                chain_free = [None, None]
                for ic in range(ICN):
                    pid = ic % 2
                    wsem = "wA" if pid == 0 else "wB"
                    lw = sch.emit("sync", lambda e, ic=ic, pid=pid: e.dma_start(
                        out=wt_bf[:, pid, :, :].rearrange("p c m -> p (c m)"), in_=d_wub[ic]),
                        waits=[wub_free[pid], wub_sp_done[ic]], inc=wsem, amt=16)
                    for tc in range(8):
                        bpid = tc % 4
                        tm = sch.emit("tensor", lambda e, tc=tc, pid=pid, bpid=bpid: [
                            e.matmul(psUG[bpid][:], wt_bf[:, pid, c, :],
                                     xr_fm[:, c, tc * 512:(tc + 1) * 512],
                                     start=(c == 0), stop=(c == DC - 1)) for c in range(DC)][-1],
                            waits=[(wsem, lw)] + (ug_free[bpid] if isinstance(ug_free[bpid], list)
                                                  else [ug_free[bpid]]), inc="tS")
                        vr = sch.emit("vector", lambda e, ic=ic, tc=tc, bpid=bpid: e.tensor_reduce(
                            out=umb[:, ic * 8 + tc: ic * 8 + tc + 1], in_=psUG[bpid][:],
                            axis=AX, op=OP.max, apply_absolute_value=True),
                            waits=[("tS", tm)], inc="vS")
                        ug_free[bpid] = [("vS", vr)]
                    wub_free[pid] = ("tS", sch.val("tS"))
                    sgp = None
                    for hf in range(2):
                        hpid = (2 * ic + hf) % 2
                        gsem = "xA" if hpid == 0 else "xB"
                        lg = sch.emit("sync", lambda e, ic=ic, hf=hf, hpid=hpid: e.dma_start(
                            out=gbuf[:, hpid, :], in_=d_gr[ic, :, hf * 2048:(hf + 1) * 2048]),
                            waits=[gb_free[hpid], gr_sp_done[ic]], inc=gsem, amt=16)
                        sa = sch.emit("scalar", lambda e, hpid=hpid: e.activation(
                            out=grb2[:, hpid, :], in_=gbuf[:, hpid, :], func=ACT.Identity,
                            bias=magicT[:], scale=SCAL[:, 2:3]),
                            waits=[(gsem, lg), bc2, chain_free[hpid]], inc="sS")
                        gb_free[hpid] = ("sS", sa)
                        sgr = sch.emit("scalar", lambda e, hpid=hpid: e.activation(
                            out=grb2[:, hpid, :], in_=grb2[:, hpid, :], func=ACT.Identity,
                            bias=nmagicT[:], scale=1.0), inc="sS")
                        ssig = sch.emit("scalar", lambda e, hpid=hpid: e.activation(
                            out=sgb2[:, hpid, :], in_=grb2[:, hpid, :], func=ACT.Sigmoid,
                            bias=0.0, scale=SCAL[:, 3:4]), inc="sS")
                        sb2 = sch.emit("scalar", lambda e, hpid=hpid: e.activation(
                            out=sgb2[:, hpid, :], in_=sgb2[:, hpid, :], func=ACT.Identity,
                            bias=magicT[:], scale=65536.0), inc="sS")
                        vsr = sch.emit("vector", lambda e, hpid=hpid: e.tensor_scalar(
                            out=sgb2[:, hpid, :], in0=sgb2[:, hpid, :], scalar1=MAGIC,
                            scalar2=65535.0, op0=OP.subtract, op1=OP.min),
                            waits=[("sS", sb2)], inc="vS")
                        ggp = sch.emit("gpsimd", lambda e, hpid=hpid: e.tensor_tensor(
                            out=gpb[:, hpid, :], in0=grb2[:, hpid, :], in1=sgb2[:, hpid, :],
                            op=OP.mult),
                            waits=[("vS", vsr), gp_free[hpid]], inc="gS")
                        chain_free[hpid] = ("gS", ggp)
                        vgm = sch.emit("vector", lambda e, ic=ic, hf=hf, hpid=hpid: e.tensor_reduce(
                            out=gpmb[:, 2 * ic + hf: 2 * ic + hf + 1], in_=gpb[:, hpid, :],
                            axis=AX, op=OP.max, apply_absolute_value=True),
                            waits=[("gS", ggp)], inc="vS")
                        sgp = sch.emit("gpsimd", lambda e, ic=ic, hf=hf, hpid=hpid: e.dma_start(
                            out=d_gp[ic, :, hf * 2048:(hf + 1) * 2048], in_=gpb[:, hpid, :]),
                            waits=[("vS", vgm)], inc="gd", amt=16)
                        gp_free[hpid] = ("gd", sgp)
                    gp_sp_done[ic] = ("gd", sgp)

                if dbg:
                    sch.emit("gpsimd", lambda e: e.dma_start(out=d_dbg_gum[1], in_=umb[:, 0:192]),
                             waits=[("vS", sch.val("vS"))], inc="gd", amt=16)
                fu = sch.emit("vector", lambda e: e.tensor_reduce(
                    out=foldt[:], in_=umb[:, 0:ICN * 8], axis=AX, op=OP.max), inc="vS")
                fw3a = fold_to_scalar(foldt[:], ("vS", fu), 0)
                fp = sch.emit("vector", lambda e: e.tensor_reduce(
                    out=foldt[:], in_=gpmb[:, 0:2 * ICN], axis=AX, op=OP.max),
                    waits=[fw3a], inc="vS")
                fw3b = fold_to_scalar(foldt[:], ("vS", fp), 1)
                b3w = collective(2, fw3b)
                def d3(e):
                    e.tensor_scalar(out=srow[:, 7:8], in0=ccs[0:1, 0:1], scalar1=srow[:, 1:2],
                                    scalar2=1.0 / 32767.0, op0=OP.mult, op1=OP.mult)
                    e.tensor_scalar(out=srow[:, 8:9], in0=ccs[0:1, 1:2], scalar1=srow[:, 3:4],
                                    scalar2=(2.0 ** -16) / 32767.0, op0=OP.mult, op1=OP.mult)
                    e.drain()
                    e.tensor_scalar(out=srow[:, 7:8], in0=srow[:, 7:8], scalar1=1e-12,
                                    scalar2=None, op0=OP.max)
                    e.tensor_scalar(out=srow[:, 8:9], in0=srow[:, 8:9], scalar1=1e-12,
                                    scalar2=None, op0=OP.max)
                    e.drain()
                    e.reciprocal(out=srow[:, 4:5], in_=srow[:, 7:8])
                    e.reciprocal(out=srow[:, 5:6], in_=srow[:, 8:9])
                    e.drain()
                    e.tensor_scalar(out=srow[:, 4:5], in0=srow[:, 4:5], scalar1=srow[:, 1:2],
                                    scalar2=None, op0=OP.mult)
                    e.tensor_scalar(out=srow[:, 5:6], in0=srow[:, 5:6], scalar1=srow[:, 3:4],
                                    scalar2=2.0 ** -16, op0=OP.mult, op1=OP.mult)
                    return e.drain()
                dv3 = sch.emit("vector", d3, waits=[b3w], inc="vS")
                bc3 = bcast_scal(("vS", dv3))
                sch.barrier()

                cut(55)
                wub_free = [("tS", sch.val("tS")), ("tS", sch.val("tS"))]
                gb_free = [("vS", sch.val("vS")), ("vS", sch.val("vS"))]
                op_free = [None, None]
                op_sp_done = [None] * ICN
                chain_free2 = [None, None]
                for ic in range(ICN):
                    pid = ic % 2
                    wsem = "wA" if pid == 0 else "wB"
                    lw = sch.emit("sync", lambda e, ic=ic, pid=pid: e.dma_start(
                        out=wt_bf[:, pid, :, :].rearrange("p c m -> p (c m)"), in_=d_wub[ic]),
                        waits=[wub_free[pid]], inc=wsem, amt=16)
                    sop = None
                    for hf in range(2):
                        hpid = (2 * ic + hf) % 2
                        gsem = "xA" if hpid == 0 else "xB"
                        lg = sch.emit("sync", lambda e, ic=ic, hf=hf, hpid=hpid: e.dma_start(
                            out=gbuf[:, hpid, :], in_=d_gp[ic, :, hf * 2048:(hf + 1) * 2048]),
                            waits=[gb_free[hpid], gp_sp_done[ic]], inc=gsem, amt=16)
                        sg2 = sch.emit("scalar", lambda e, hpid=hpid: e.activation(
                            out=sgb2[:, hpid, :], in_=gbuf[:, hpid, :], func=ACT.Identity,
                            bias=magicT[:], scale=SCAL[:, 5:6]),
                            waits=[(gsem, lg), bc3, chain_free2[hpid]], inc="sS")
                        vg2 = sch.emit("vector", lambda e, hpid=hpid: e.tensor_scalar(
                            out=sgb2[:, hpid, :], in0=sgb2[:, hpid, :], scalar1=MAGIC,
                            scalar2=None, op0=OP.subtract),
                            waits=[("sS", sg2)], inc="vS")
                        gb_free[hpid] = ("vS", vg2)
                        vlast = None
                        for tci in range(4):
                            tc = hf * 4 + tci
                            bpid = tc % 4
                            tm = sch.emit("tensor", lambda e, tc=tc, pid=pid, bpid=bpid: [
                                e.matmul(psUG[bpid][:], wt_bf[:, pid, c, :],
                                         xr_fm[:, c, tc * 512:(tc + 1) * 512],
                                         start=(c == 0), stop=(c == DC - 1)) for c in range(DC)][-1],
                                waits=[(wsem, lw)] + (ug_free[bpid] if isinstance(ug_free[bpid], list)
                                                      else [ug_free[bpid]]), inc="tS")
                            s1 = sch.emit("scalar", lambda e, tci=tci, hpid=hpid, bpid=bpid:
                                          e.activation(
                                out=grb2[:, hpid, tci * 512:(tci + 1) * 512], in_=psUG[bpid][:],
                                func=ACT.Identity, bias=magicT[:], scale=SCAL[:, 4:5]),
                                waits=[("tS", tm), chain_free2[hpid]], inc="sS")
                            vlast = sch.emit("vector", lambda e, tci=tci, hpid=hpid: e.tensor_scalar(
                                out=grb2[:, hpid, tci * 512:(tci + 1) * 512],
                                in0=grb2[:, hpid, tci * 512:(tci + 1) * 512],
                                scalar1=MAGIC, scalar2=None, op0=OP.subtract),
                                waits=[("sS", s1)], inc="vS")
                            ug_free[bpid] = [("sS", s1)]
                        gop = sch.emit("gpsimd", lambda e, hpid=hpid: e.tensor_tensor(
                            out=gpb[:, hpid, :], in0=sgb2[:, hpid, :], in1=grb2[:, hpid, :],
                            op=OP.mult),
                            waits=[("vS", vlast), op_free[hpid]], inc="gS")
                        chain_free2[hpid] = ("gS", gop)
                        vom = sch.emit("vector", lambda e, ic=ic, hf=hf, hpid=hpid: e.tensor_reduce(
                            out=opmb[:, 2 * ic + hf: 2 * ic + hf + 1], in_=gpb[:, hpid, :],
                            axis=AX, op=OP.max, apply_absolute_value=True),
                            waits=[("gS", gop)], inc="vS")
                        sop = sch.emit("gpsimd", lambda e, ic=ic, hf=hf, hpid=hpid: e.dma_start(
                            out=d_op[hf * 8:(hf + 1) * 8, :, ic, :].rearrange(
                                "b p t -> p b t"),
                            in_=gpb[:, hpid, :].rearrange("p (b t) -> p b t", t=256)),
                            waits=[("vS", vom)], inc="gd", amt=16)
                        op_free[hpid] = ("gd", sop)
                    wub_free[pid] = ("tS", sch.val("tS"))
                    op_sp_done[ic] = ("gd", sop)

                fo = sch.emit("vector", lambda e: e.tensor_reduce(
                    out=foldt[:], in_=opmb[:, 0:2 * ICN], axis=AX, op=OP.max), inc="vS")
                fw4 = fold_to_scalar(foldt[:], ("vS", fo), 0)
                b4w = collective(3, fw4)
                def d4(e):
                    e.tensor_scalar(out=srow[:, 9:10], in0=srow[:, 7:8], scalar1=srow[:, 8:9],
                                    scalar2=None, op0=OP.mult)
                    e.drain()
                    e.tensor_scalar(out=srow[:, 10:11], in0=ccs[0:1, 0:1], scalar1=srow[:, 9:10],
                                    scalar2=1.0 / 32767.0, op0=OP.mult, op1=OP.mult)
                    e.drain()
                    e.tensor_scalar(out=srow[:, 10:11], in0=srow[:, 10:11], scalar1=1e-12,
                                    scalar2=None, op0=OP.max)
                    e.drain()
                    e.reciprocal(out=srow[:, 6:7], in_=srow[:, 10:11])
                    e.drain()
                    e.tensor_scalar(out=srow[:, 6:7], in0=srow[:, 6:7], scalar1=srow[:, 9:10],
                                    scalar2=None, op0=OP.mult)
                    return e.drain()
                dv4 = sch.emit("vector", d4, waits=[b4w], inc="vS")
                bc4 = bcast_scal(("vS", dv4))
                sch.barrier()
            op_sd = ("gd", sch.val("gd"))

            cut(60)
            with contextlib.ExitStack() as es5:
                E5 = es5.enter_context
                wdq_raw = E5(nc.sbuf_tensor("wdq_raw", [128, II], F32))
                wd_bmax = E5(nc.sbuf_tensor("wd_bmax", [128, 96], F32))
                wd_binv = E5(nc.sbuf_tensor("wd_binv", [128, 96], F32))
                wd_q1 = E5(nc.sbuf_tensor("wd_q1", [128, II], F32))
                wd_qb = E5(nc.sbuf_tensor("wd_qb", [128, II], BF16))
                wdt_bf = E5(nc.sbuf_tensor("wdt_bf", [128, ICN, D], BF16))
                opst = E5(nc.sbuf_tensor("opst", [128, 2, ICN, 256], F32))
                orb = E5(nc.sbuf_tensor("orb", [128, 2, ICN, 256], BF16))
                oa = E5(nc.sbuf_tensor("oa", [128, 2, 256], F32))
                lres = E5(nc.sbuf_tensor("lres", [128, 2, D], F32))
                otile = E5(nc.sbuf_tensor("otile", [128, 2, D], F32))
                bnb2 = E5(nc.sbuf_tensor("bnb2", [128, 2, 2, 6], F32))
                mvb2 = E5(nc.sbuf_tensor("mvb2", [128, 2, 2, 2], F32))
                lnsc2 = E5(nc.sbuf_tensor("lnsc2", [128, 2, 6], F32))
                meanb2 = E5(nc.sbuf_tensor("meanb2", [128, 2], F32))
                varb2 = E5(nc.sbuf_tensor("varb2", [128, 2], F32))
                rstdb2 = E5(nc.sbuf_tensor("rstdb2", [128, 2], F32))
                psWD = [E5(nc.psum_tensor(un(f"psWD{i}"), [128, 8, 128], BF16)) for i in range(2)]
                psDN = [E5(nc.psum_tensor(un(f"psDN{i}"), [128, 2, 512], F32)) for i in range(2)]

                wdr_free = [None]
                wdq1_free = [None]
                wdps_free = [None, None]
                wdqb_free = [None, None]
                for ht in range(DC):
                    pid = ht % 2
                    lv = sch.emit("sync", lambda e, ht=ht: e.dma_start(
                        out=wdq_raw[:], in_=dw["wdown"][ht]),
                        waits=[wdr_free[0]], inc="wA", amt=16)
                    def vq(e):
                        e.tensor_reduce(out=wd_bmax[:],
                                        in_=wdq_raw[:].rearrange("p (b k) -> p b k", k=32),
                                        axis=AX, op=OP.max, apply_absolute_value=True)
                        e.drain()
                        e.tensor_scalar(out=wd_bmax[:], in0=wd_bmax[:],
                                        scalar1=1.0 / 7.0, scalar2=1e-12, op0=OP.mult, op1=OP.max)
                        e.drain()
                        e.reciprocal(out=wd_binv[:], in_=wd_bmax[:])
                        e.drain()
                        return e.tensor_tensor(
                            out=wd_q1[:].rearrange("p (b k) -> p b k", k=32),
                            in0=wdq_raw[:].rearrange("p (b k) -> p b k", k=32),
                            in1=wd_binv[:, :, None].broadcast_to((128, 96, 32)), op=OP.mult)
                    v1 = sch.emit("vector", vq, waits=[("wA", lv), wdq1_free[0]], inc="vS")
                    s1 = sch.emit("scalar", lambda e: e.activation(
                        out=wd_q1[:], in_=wd_q1[:], func=ACT.Identity, bias=magicT[:], scale=1.0),
                        waits=[("vS", v1)], inc="sS")
                    s2 = sch.emit("scalar", lambda e: e.activation(
                        out=wd_q1[:], in_=wd_q1[:], func=ACT.Identity, bias=nmagicT[:], scale=1.0),
                        inc="sS")
                    v2 = sch.emit("vector", lambda e: e.tensor_tensor(
                        out=wd_qb[:].rearrange("p (b k) -> p b k", k=32),
                        in0=wd_q1[:].rearrange("p (b k) -> p b k", k=32),
                        in1=wd_bmax[:, :, None].broadcast_to((128, 96, 32)), op=OP.mult),
                        waits=[("sS", s2), wdqb_free[0]], inc="vS")
                    wdq1_free[0] = ("vS", v2)
                    wdr_free[0] = ("vS", v1)
                    last_tv = None
                    for icg in range(3):
                        tpid = icg % 2
                        tv = sch.emit("tensor", lambda e, icg=icg, tpid=tpid: [
                            e.transpose(psWD[tpid][:, c, :],
                                        wd_qb[:, (icg * 8 + c) * 128:(icg * 8 + c + 1) * 128],
                                        ident_bf[:]) for c in range(8)][-1],
                            waits=[("vS", v2), wdps_free[tpid]], inc="tS")
                        vv = sch.emit("vector", lambda e, ht=ht, icg=icg, tpid=tpid: e.tensor_copy(
                            out=wdt_bf[:, icg * 8:(icg + 1) * 8, ht * 128:(ht + 1) * 128],
                            in_=psWD[tpid][:]),
                            waits=[("tS", tv)], inc="vS")
                        wdps_free[tpid] = ("vS", vv)
                        last_tv = tv
                    wdqb_free[0] = ("tS", last_tv)
                wd_done = ("vS", sch.val("vS"))

                ops_free = [None, None]
                lres_free = [None, None]
                dn_free = [None, None]
                ot_free = [None, None]
                oa_free = [None, None]
                orb_free = [None, None]
                ord_mark = {}
                NB = 16

                def rounds(tb):
                    pid = tb % 2
                    osem = "xA" if pid == 0 else "xB"
                    lo = sch.emit("sync", lambda e, tb=tb, pid=pid: e.dma_start(
                        out=opst[:, pid, :, :], in_=d_op[tb]),
                        waits=[ops_free[pid], op_sd], inc=osem, amt=16)
                    for ic in range(ICN):
                        apid = ic % 2
                        s1 = sch.emit("scalar", lambda e, ic=ic, pid=pid, apid=apid: e.activation(
                            out=oa[:, apid, :], in_=opst[:, pid, ic, :], func=ACT.Identity,
                            bias=magicT[:], scale=SCAL[:, 6:7]),
                            waits=[(osem, lo), bc4, oa_free[apid]], inc="sS")
                        v1 = sch.emit("vector", lambda e, ic=ic, pid=pid, apid=apid: e.tensor_scalar(
                            out=orb[:, pid, ic, :], in0=oa[:, apid, :], scalar1=MAGIC, scalar2=None,
                            op0=OP.subtract),
                            waits=[("sS", s1), orb_free[pid] if ic == 0 else None], inc="vS")
                        oa_free[apid] = ("vS", v1)
                    ops_free[pid] = ("sS", sch.val("sS"))
                    ord_mark[tb] = ("vS", sch.val("vS"))

                rounds(0)
                for tb in range(NB):
                    pid = tb % 2
                    if tb + 1 < NB:
                        rounds(tb + 1)
                    if dbg and tb == 0:
                        sch.emit("gpsimd", lambda e: e.dma_start(out=d_dbg_or, in_=orb[:, 0]),
                                 waits=[ord_mark[0]], inc="gd", amt=16)
                    for ti in range(2):
                        tt32 = tb * 2 + ti
                        dpid = ti % 2
                        rsem = "pA" if dpid == 0 else "pB"
                        lr = sch.emit("sync", lambda e, tt32=tt32, dpid=dpid: e.dma_start(
                            out=lres[:, dpid, :], in_=d_ln1[tt32]),
                            waits=[lres_free[dpid], ln1_sd], inc=rsem, amt=16)
                        tdn = sch.emit("tensor", lambda e, ti=ti, pid=pid, dpid=dpid: [
                            e.matmul(psDN[dpid][:, nh, :], orb[:, pid, ic, ti * 128:(ti + 1) * 128],
                                     wdt_bf[:, ic, nh * 512:(nh + 1) * 512],
                                     start=(ic == 0), stop=(ic == ICN - 1))
                            for nh in range(2) for ic in range(ICN)][-1],
                            waits=[ord_mark[tb], wd_done, dn_free[dpid]], inc="tS")
                        vres = sch.emit("vector", lambda e, ti=ti, dpid=dpid: [
                            e.scalar_tensor_tensor(
                                out=otile[:, ti, nh * 512:(nh + 1) * 512],
                                in0=psDN[dpid][:, nh, :], scalar=SCAL[:, 10:11],
                                in1=lres[:, dpid, nh * 512:(nh + 1) * 512],
                                op0=OP.mult, op1=OP.add) for nh in range(2)][-1],
                            waits=[(rsem, lr), ("tS", tdn), ot_free[ti]], inc="vS")
                        dn_free[dpid] = ("vS", vres)
                        lres_free[dpid] = ("vS", vres)
                        if dbg and tb == 0:
                            sch.emit("gpsimd", lambda e, ti=ti: e.dma_start(
                                out=d_dbg_ffn[ti], in_=otile[:, ti, :]),
                                waits=[("vS", sch.val("vS"))], inc="gd", amt=16)
                        sch.emit("vector", lambda e, ti=ti: [
                            e.bn_stats(out=bnb2[:, ti, 0, :], in_=otile[:, ti, 0:512]),
                            e.bn_stats(out=bnb2[:, ti, 1, :], in_=otile[:, ti, 512:1024])][-1],
                            inc="vS")
                    orb_free[pid] = ("tS", sch.val("tS"))
                    def combine2(e):
                        for ti in range(2):
                            e.bn_aggr(out=mvb2[:, ti, 0, :], in_=bnb2[:, ti, 0, :])
                            e.bn_aggr(out=mvb2[:, ti, 1, :], in_=bnb2[:, ti, 1, :])
                        e.drain()
                        e.tensor_tensor(out=lnsc2[:, :, 0], in0=mvb2[:, :, 0, 0],
                                        in1=mvb2[:, :, 1, 0], op=OP.subtract)
                        e.tensor_tensor(out=lnsc2[:, :, 2], in0=mvb2[:, :, 0, 0],
                                        in1=mvb2[:, :, 1, 0], op=OP.add)
                        e.tensor_tensor(out=lnsc2[:, :, 3], in0=mvb2[:, :, 0, 1],
                                        in1=mvb2[:, :, 1, 1], op=OP.add)
                        e.drain()
                        e.tensor_scalar(out=lnsc2[:, :, 1], in0=lnsc2[:, :, 0], scalar1=0.5,
                                        scalar2=None, op0=OP.mult)
                        e.tensor_scalar(out=meanb2[:], in0=lnsc2[:, :, 2], scalar1=0.5,
                                        scalar2=None, op0=OP.mult)
                        e.drain()
                        e.tensor_tensor(out=lnsc2[:, :, 4], in0=lnsc2[:, :, 1], in1=lnsc2[:, :, 1],
                                        op=OP.mult)
                        e.drain()
                        return e.scalar_tensor_tensor(
                            out=varb2[:], in0=lnsc2[:, :, 3], scalar=0.5,
                            in1=lnsc2[:, :, 4], op0=OP.mult, op1=OP.add)
                    cv2 = sch.emit("vector", combine2, inc="vS")
                    sq2 = sch.emit("scalar", lambda e: e.activation(
                        out=varb2[:], in_=varb2[:], func=ACT.Sqrt, bias=epsT[:], scale=1.0),
                        waits=[("vS", cv2)], inc="sS")
                    sch.emit("vector", lambda e: e.reciprocal(out=rstdb2[:], in_=varb2[:]),
                             waits=[("sS", sq2)], inc="vS")
                    dr2b = sch.emit("vector", lambda e: e.drain(), inc="vS")
                    for ti in range(2):
                        tt32 = tb * 2 + ti
                        b_i, trow = divmod(tt32, 16)
                        nv = sch.emit("vector", lambda e, ti=ti: e.tensor_scalar(
                            out=otile[:, ti, :], in0=otile[:, ti, :],
                            scalar1=meanb2[:, ti:ti + 1], scalar2=rstdb2[:, ti:ti + 1],
                            op0=OP.subtract, op1=OP.mult), waits=[("vS", dr2b)], inc="vS")
                        sch.emit("gpsimd", lambda e, ti=ti, b_i=b_i, trow=trow: e.dma_start(
                            out=d_out[b_i, trow * 128:(trow + 1) * 128, :], in_=otile[:, ti, :]),
                            waits=[("vS", nv)], inc="so", amt=16)
                        ot_free[ti] = ("so", sch.val("so"))

                if dbg:
                    sch.emit("gpsimd", lambda e: e.dma_start(out=d_dbg_scal, in_=srow[:]),
                             waits=[("vS", sch.val("vS"))], inc="gd", amt=16)
                fin = [("so", sch.val("so")), ("gd", sch.val("gd")), ("sd", sch.val("sd"))]
                sch.emit("sync", lambda e: e.nop(), waits=fin)


        except _Cut:
            pass
        if sch.cutoff is not None:
            co = sch.cutoff
            tot = {s: 0 for s in ALLSEMS}
            for e in ENGS:
                kept = [op for op in sch.ops[e] if op[0] < co]
                sch.ops[e] = kept
                for _, _, inc, amt, _f in kept:
                    if inc is not None:
                        tot[inc] += amt
            sch.cnt = tot
        sch.emit("sync", lambda e: e.nop(),
                 waits=[(s, sch.val(s)) for s in ("ld", "sd", "gd", "so", "cc",
                                                  "tS", "vS", "sS", "gS",
                                                  "xA", "xB", "wA", "wB", "pA", "pB")])

        with nc.Block() as block:
            def mk(engname):
                def body(e):
                    for _seq, w, inc, amt, fn in sch.ops[engname]:
                        for (s, v) in w:
                            e.wait_ge(sems[s], v)
                        ins = fn(e)
                        if inc is not None:
                            if not hasattr(ins, "then_inc"):
                                ins = e.nop()
                            ins.then_inc(sems[inc], amt)
                return body
            block.tensor(mk("tensor"))
            block.vector(mk("vector"))
            block.scalar(mk("scalar"))
            block.gpsimd(mk("gpsimd"))
            block.sync(mk("sync"))
    return nc


def _host_prep(inputs):
    g = lambda k: np.asarray(inputs[k])
    x = np.ascontiguousarray(np.asarray(g("L0_attn_x"), np.float32))
    wup = np.ascontiguousarray(np.asarray(g("L0_ffn_w_up"), np.float32).reshape(ICN, 128, D))
    wgate = np.ascontiguousarray(np.asarray(g("L0_ffn_w_gate"), np.float32).reshape(ICN, 128, D))
    wdown = np.ascontiguousarray(np.asarray(g("L0_ffn_w_down"), np.float32).reshape(DC, 128, II))
    ident = np.eye(128, dtype=np.float32)
    r16 = np.zeros((128, S), dtype=np.float32)
    r16[np.arange(S) // 16, np.arange(S)] = 1.0
    b2 = lambda v: np.ascontiguousarray(np.asarray(v, np.float32).reshape(EC, 128).T)
    common = {
        "wi": g("L0_attn_Wi"), "wq": g("L0_attn_Wq"), "wk": g("L0_attn_Wk"),
        "wv": g("L0_attn_Wv"), "wp": g("L0_attn_Wp"), "w1": g("L0_attn_W1"),
        "w2": g("L0_attn_W2"), "wo": g("L0_attn_Wo"),
        "wh2": np.ascontiguousarray(np.asarray(g("L0_attn_Wh"), np.float32)[:, 2]
                                    .reshape(EC, 128).T),
        "cwT": np.asarray(g("L0_attn_conv_w"), np.float32).T,
        "bi": b2(g("L0_attn_bi")), "bq": b2(g("L0_attn_bq")), "bk": b2(g("L0_attn_bk")),
        "bv1": np.asarray(g("L0_attn_bv"), np.float32).reshape(1, E),
        "bp": b2(g("L0_attn_bp")), "b1": b2(g("L0_attn_b1")), "b2": b2(g("L0_attn_b2")),
        "bo1": np.asarray(g("L0_attn_bo"), np.float32).reshape(1, D),
        "cb": b2(g("L0_attn_conv_b")),
        "bh2": np.asarray(g("L0_attn_bh"), np.float32)[2].reshape(1, 1),
        "wup": wup, "wgate": wgate, "wdown": wdown,
        "ident": ident, "r16": r16,
    }
    common = {k: np.ascontiguousarray(np.asarray(v, np.float32)) for k, v in common.items()}
    maps = []
    for c in range(NCORES):
        m = dict(common)
        m["x"] = np.ascontiguousarray(x[B * c:B * (c + 1)])
        maps.append(m)
    return maps


_NC_CACHE = {}


def kernel(**inputs):
    maps = _host_prep(inputs)
    if "nc" not in _NC_CACHE:
        _NC_CACHE["nc"] = build_nc(dbg=False)
    nc = _NC_CACHE["nc"]
    res = run_bass_kernel_spmd(nc, maps, core_ids=list(range(NCORES)))
    out = np.concatenate([np.asarray(r["out"]) for r in res.results], axis=0)
    return np.ascontiguousarray(out.astype(np.float32))
